# version 25
# baseline (speedup 1.0000x reference)
import numpy as np
import ml_dtypes
from contextlib import ExitStack

import concourse.bass as bass
import concourse.mybir as mybir
from concourse.bass_utils import run_bass_kernel_spmd

F32 = mybir.dt.float32
BF16 = mybir.dt.bfloat16
AF = mybir.ActivationFunctionType
ALU = mybir.AluOpType
AX = mybir.AxisListType

NT = 16
ND = 16
DM = 2048
H, KVH, DH = 16, 4, 128
SINK = 640.0
LNC = float(np.log(128.0 ** -0.25))
ENGS = ("sp", "gp", "pe", "ac", "dv")


class _Null:
    def __getattr__(self, k):
        def f(*a, **kw):
            return None
        return f


class Ctx:
    def __init__(self):
        self.mode = "count"
        self.cnt = {}
        self.labels = {}
        self.handles = {}
        self.waited = set()

    def reset(self):
        for k in list(self.cnt):
            self.cnt[k] = 0

    def tick(self, ins, sem, n=1, label=None):
        self.cnt[sem] += n
        if self.mode == "emit" and ins is not None:
            ins.then_inc(self.handles[sem], n)
        if label is not None:
            key = (sem, label)
            if self.mode == "count":
                assert key not in self.labels, f"dup label {key}"
                self.labels[key] = self.cnt[sem]
            else:
                assert self.labels[key] == self.cnt[sem], \
                    f"label drift {key}: {self.labels[key]} vs {self.cnt[sem]}"

    def tickd(self, E, sem, label=None):
        ins = E.drain()
        self.tick(ins, sem, 1, label)

    def wait(self, E, sem, label):
        self.waited.add((sem, label))
        if self.mode == "emit":
            E.wait_ge(self.handles[sem], self.labels[(sem, label)])

    def check(self):
        missing = [k for k in self.waited if k not in self.labels]
        assert not missing, f"missing labels: {sorted(missing)[:8]}"


def build_nc():
    nc = bass.Bass("TRN2", target_bir_lowering=False)

    dram = {}
    dram["x"] = nc.dram_tensor("x", [DM, DM], F32, kind="ExternalInput").ap()
    dram["y"] = nc.dram_tensor("y", [DM, DM], F32, kind="ExternalOutput").ap()
    dram["cos"] = nc.dram_tensor("cos", [DM, DH], F32, kind="ExternalInput").ap()
    dram["sin"] = nc.dram_tensor("sin", [DM, DH], F32, kind="ExternalInput").ap()
    for nm, sh in (("wq", [DM, DM]), ("wk", [DM, 512]), ("wv", [DM, 512]),
                   ("wo", [DM, DM]), ("wupT", [DM, DM]), ("wdownT", [DM, DM]),
                   ("wexp", [DM, DM])):
        dram[nm] = nc.dram_tensor(nm, sh, F32, kind="ExternalInput").ap()
    dram["qn"] = nc.dram_tensor("qn", [1, DH], F32, kind="ExternalInput").ap()
    dram["kn"] = nc.dram_tensor("kn", [1, DH], F32, kind="ExternalInput").ap()
    dram["identbf"] = nc.dram_tensor("identbf", [128, 128], BF16, kind="ExternalInput").ap()
    dram["ident32"] = nc.dram_tensor("ident32", [128, 128], F32, kind="ExternalInput").ap()

    c = Ctx()
    SEMS = ("ini", "xe", "xo", "scs", "w0", "w1", "w2", "w3",
            "st0", "st1", "pe", "ac", "dv", "xp0", "xp1", "xp2", "xp3")
    for s in SEMS:
        c.cnt[s] = 0

    with ExitStack() as stack:
        e = stack.enter_context
        A = e(nc.sbuf_tensor("arenaA", [128, 16384], F32))
        B = e(nc.sbuf_tensor("arenaB", [128, 16384], F32))
        C = e(nc.sbuf_tensor("arenaC", [128, 16384], F32))
        S = e(nc.sbuf_tensor("arenaS", [128, 2080], F32))
        negm_all = e(nc.sbuf_tensor("negm_all", [128, 256], F32))
        qn_bc = e(nc.sbuf_tensor("qn_bc", [128, 128], F32))
        kn_bc = e(nc.sbuf_tensor("kn_bc", [128, 128], F32))
        identbf = e(nc.sbuf_tensor("identbf_sb", [128, 128], BF16))
        ident32 = e(nc.sbuf_tensor("ident32_sb", [128, 128], F32))
        ones = e(nc.sbuf_tensor("ones_sb", [1, 128], F32))
        onescol = e(nc.sbuf_tensor("onescol_sb", [128, 2], BF16))
        ones1b = e(nc.sbuf_tensor("ones1b_sb", [1, 128], BF16))
        epsc = e(nc.sbuf_tensor("epsc", [128, 4], F32))
        stt = e(nc.sbuf_tensor("stats", [128, 64], F32))
        ssbk_all = e(nc.sbuf_tensor("ssbk_all", [128, 64], F32))
        mkh = e(nc.sbuf_tensor("mkh", [128, 4], F32))
        junkA = e(nc.sbuf_tensor("junkA", [128, 512], F32))
        e_all = e(nc.sbuf_tensor("e_all", [128, 256], F32))
        ps_big = e(nc.psum_tensor("ps_big", [128, 2048], F32))
        ps_tt = e(nc.psum_tensor("ps_tt", [128, 2048], BF16))
        ps_a = e(nc.psum_tensor("ps_a", [128, 512], F32))
        ps_b = e(nc.psum_tensor("ps_b", [128, 512], F32))

        pstt32 = ps_tt.ap().bitcast(F32)

        def ring_ap(r):
            if r == 0:
                return ps_a.ap()[:]
            if r == 1:
                return ps_b.ap()[:]
            if r == 2:
                return ps_big.ap()[:, 0:512]
            return ps_big.ap()[:, 512:1024]

        def pst_ap(parity):
            return ps_tt.ap()[:, parity * 1024: parity * 1024 + 1024]
        for s in SEMS:
            c.handles[s] = e(nc.semaphore(f"sem_{s}"))
        block = e(nc.Block())

        A32, B32, C32, S32 = A.ap(), B.ap(), C.ap(), S.ap()
        A_bf = A32.bitcast(BF16)
        B_bf = B32.bitcast(BF16)
        C_bf = C32.bitcast(BF16)
        S_bf = S32.bitcast(BF16)

        def col(t, j, w=1):
            return t.ap()[:, j:j + w]

        qn_rowV = junkA.ap()[0:1, 256:384]
        kn_rowV = junkA.ap()[0:1, 384:512]
        tr64V = junkA.ap()[0:64, 0:128]
        t2cV = junkA.ap()[0:64, 192:193]
        t3rV = junkA.ap()[0:1, 128:192]
        mkn2V = junkA.ap()[0:1, 200:204]

        def hT_lhsT(dc, tt):
            return A_bf[:, dc * 2048 + tt * 128: dc * 2048 + (tt + 1) * 128]

        def hT_rhs(dc, tc):
            return A_bf[:, dc * 2048 + tc * 512: dc * 2048 + (tc + 1) * 512]

        def AT_lhsT(hc, tt):
            return A_bf[:, hc * 2048 + tt * 128: hc * 2048 + (tt + 1) * 128]

        def uT_lhsT(uc, tt):
            return B_bf[:, uc * 2048 + tt * 128: uc * 2048 + (tt + 1) * 128]

        def qT_lhsT(h, qg):
            return B_bf[:, h * 2048 + qg * 128: h * 2048 + (qg + 1) * 128]

        def cslot_bf(k):
            return C_bf[:, k * 8192:(k + 1) * 8192]

        def kT_rhs(kvh, kc):
            b0 = 16384 + kvh * 2048 + kc * 512
            return C_bf[:, b0: b0 + 512]

        def vt_lhsT(kc, kvh):
            b0 = 24576 + kc * 512 + kvh * 128
            return C_bf[:, b0: b0 + 128]

        def p_slot(par):
            return C_bf[:, par * 2048:(par + 1) * 2048]

        def pT_rhs(kc):
            return C_bf[:, 4096 + kc * 512: 4096 + (kc + 1) * 512]

        def xsrc(sub):
            return dram["x"] if sub == 0 else dram["y"]

        y_view = dram["y"].rearrange("(t p) o -> p t o", p=128)

        def ph_init(en, E):
            if en == "sp":
                dsts = (("identbf", identbf.ap()[:]), ("ident32", ident32.ap()[:]),
                        ("qn", qn_rowV), ("kn", kn_rowV))
                for i, (nm, dst) in enumerate(dsts):
                    ins = E.dma_start(dst, dram[nm])
                    c.tick(ins, "ini", 16, "ld_all" if i == 3 else None)
            if en == "dv":
                E.memset(ones.ap()[:], 1.0)
                E.memset(onescol.ap()[:], 1.0)
                E.memset(ones1b.ap()[:], 1.0)
                E.memset(col(epsc, 0), 1e-5)
                E.memset(col(epsc, 1), 1e-6)
                E.memset(col(epsc, 2), LNC)
                c.tickd(E, "dv", "init_consts")
            if en == "pe":
                c.wait(E, "ini", "ld_all")
                c.wait(E, "dv", "init_consts")
                E.matmul(ps_b.ap()[:, 0:128], ones.ap()[:], qn_rowV,
                         start=True, stop=True)
                ins = E.matmul(ps_b.ap()[:, 128:256], ones.ap()[:], kn_rowV,
                               start=True, stop=True)
                c.tick(ins, "pe", 1, "gains_mm")
            if en == "ac":
                c.wait(E, "pe", "gains_mm")
                E.activation(qn_bc.ap()[:], ps_b.ap()[:, 0:128], AF.Identity)
                E.activation(kn_bc.ap()[:], ps_b.ap()[:, 128:256], AF.Identity)
                c.tickd(E, "ac", "gains_bc")

        def ph_pre0(en, E):
            if en == "sp":
                for tt in range(4):
                    ins = E.dma_start(C32[:, 8192 + tt * 2048: 8192 + (tt + 1) * 2048],
                                      dram["x"][tt * 128:(tt + 1) * 128, :])
                    c.tick(ins, f"xp{tt}", 16, f"s0_x{tt}")

        def ph_copy(en, E):
            if en == "sp":
                for i in range(4):
                    ins = E.dma_start(dram["y"][i * 512:(i + 1) * 512, :],
                                      dram["x"][i * 512:(i + 1) * 512, :])
                    c.tick(ins, "ini", 16, "xy_copy3" if i == 3 else None)

        def ph_hT(sub, en, E):
            tag = f"s{sub}"
            src = xsrc(sub)
            npre = 4 if sub in (0, 1) else 0

            def xbuf(tt):
                if tt < npre:
                    return C32[:, 8192 + tt * 2048: 8192 + (tt + 1) * 2048]
                b0 = (0, 2048, 6144, 8192)[tt % 4]
                return B32[:, b0: b0 + 2048]

            def hbuf(tt):
                b0 = 8192 + (tt % 2) * 2048
                return B_bf[:, b0: b0 + 2048]

            for tt in range(NT):
                sb = (tt % 4) * 8
                if en == "sp":
                    if sub == 1 and tt >= npre:
                        c.wait(E, "st0", "s0_acc_done")
                        c.wait(E, "st1", "s0_acc_done2")
                    if sub == 2:
                        for j4 in range(4):
                            kk = j4 * NT + tt
                            pr = kk - kk % 2
                            c.wait(E, f"st{(pr // 2) % 2}", f"a6_stk{pr}")
                    if tt >= npre:
                        if tt >= 4 + npre:
                            c.wait(E, "ac", f"{tag}_happ{tt-4}")
                        ins = E.dma_start(xbuf(tt), src[tt * 128:(tt + 1) * 128, :])
                        c.tick(ins, "xe" if tt % 2 == 0 else "xo", 16, f"{tag}_x{tt}")
                if en == "dv":
                    xs = (f"xp{tt}" if tt < npre else ("xe" if tt % 2 == 0 else "xo"))
                    c.wait(E, xs, f"{tag}_x{tt}")
                    if tt >= 4:
                        c.wait(E, "ac", f"{tag}_happ{tt-4}")
                    for i4 in range(4):
                        E.bn_stats(junkA.ap()[:, i4 * 6:(i4 + 1) * 6],
                                   xbuf(tt)[:, i4 * 512:(i4 + 1) * 512])
                    E.drain()
                    E.bn_aggr(col(stt, sb + 0, 2), junkA.ap()[:, 0:24])
                    c.tickd(E, "dv", f"{tag}_ssum{tt}")
                    if tt >= 1:
                        for g in range(4):
                            gid = (tt - 1) * 4 + g
                            gp2 = gid % 2
                            c.wait(E, "pe", f"{tag}_hT{tt-1}g{g}")
                            dst = A_bf.rearrange("p (a b) -> p a b", b=2048)[
                                :, g * 4:(g + 1) * 4, (tt - 1) * 128:tt * 128]
                            src_ps = pst_ap(gp2)[:, 0:512].rearrange(
                                "p (a b) -> p a b", b=128)
                            ins = E.tensor_copy(dst, src_ps)
                            c.tick(ins, "dv", 1, f"{tag}_hTe{gid}")
                if en == "ac" and tt % 2 == 0:
                    pair = (tt, tt + 1)
                    for u in pair:
                        c.wait(E, "dv", f"{tag}_ssum{u}")
                    for u in pair:
                        sbu = (u % 4) * 8
                        E.activation(col(stt, sbu + 7), col(stt, sbu + 1), AF.Ln,
                                     bias=col(epsc, 0))
                    E.drain()
                    for u in pair:
                        sbu = (u % 4) * 8
                        E.activation(col(stt, sbu + 6), col(stt, sbu + 7), AF.Exp,
                                     scale=-0.5)
                    E.drain()
                    for u in pair:
                        sbu = (u % 4) * 8
                        E.activation(col(stt, sbu + 3), col(stt, sbu + 0),
                                     AF.Identity, scale=col(stt, sbu + 6))
                    E.drain()
                    for u in pair:
                        sbu = (u % 4) * 8
                        E.activation(col(stt, sbu + 5), col(stt, sbu + 3),
                                     AF.Identity, scale=-1.0)
                    E.drain()
                    for u in pair:
                        sbu = (u % 4) * 8
                        if u >= 2:
                            c.wait(E, "pe", f"{tag}_hT{u-2}g3")
                        for hc4 in range(4):
                            ins = E.activation(hbuf(u)[:, hc4 * 512:(hc4 + 1) * 512],
                                               xbuf(u)[:, hc4 * 512:(hc4 + 1) * 512],
                                               AF.Identity,
                                               scale=col(stt, sbu + 6),
                                               bias=col(stt, sbu + 5))
                            c.tick(ins, "ac", 1, f"{tag}_happ{u}c{hc4}")
                        c.tick(None, "ac", 0, f"{tag}_happ{u}")
                if en == "pe":
                    for g in range(4):
                        gid = tt * 4 + g
                        gp2 = gid % 2
                        c.wait(E, "ac", f"{tag}_happ{tt}c{g}")
                        if gid >= 2:
                            c.wait(E, "dv", f"{tag}_hTe{gid-2}")
                        ins = None
                        for j in range(4):
                            dc = g * 4 + j
                            ins = E.transpose(
                                pst_ap(gp2)[:, j * 128:(j + 1) * 128],
                                hbuf(tt)[:, dc * 128:(dc + 1) * 128],
                                identbf.ap()[:])
                        c.tick(ins, "pe", 1, f"{tag}_hT{tt}g{g}")
            if en == "dv":
                for g in range(4):
                    gid = (NT - 1) * 4 + g
                    gp2 = gid % 2
                    c.wait(E, "pe", f"{tag}_hT{NT-1}g{g}")
                    dst = A_bf.rearrange("p (a b) -> p a b", b=2048)[
                        :, g * 4:(g + 1) * 4, (NT - 1) * 128:NT * 128]
                    src_ps = pst_ap(gp2)[:, 0:512].rearrange(
                        "p (a b) -> p a b", b=128)
                    ins = E.tensor_copy(dst, src_ps)
                    c.tick(ins, "dv", 1, f"{tag}_hTe{gid}")
                c.tick(None, "dv", 0, f"{tag}_hT_done")

        def ph_up(sub, en, E):
            tag = f"s{sub}u"
            wsrc = dram["wupT"].rearrange("(dc p) o -> p dc o", p=128)

            def up_ring(ot, tc):
                if ot % 2 == 0:
                    return ps_big.ap()[:, tc * 512:(tc + 1) * 512]
                return (ps_a.ap()[:], ps_b.ap()[:],
                        pstt32[:, 0:512], pstt32[:, 512:1024])[tc]
            for q in range(4):
                slot = q % 2
                if en == "gp":
                    if q >= 2:
                        c.wait(E, "pe", f"{tag}_q{q-2}done")
                    if sub == 2 and q < 2:
                        c.wait(E, "pe", "a5_done")
                    dst = cslot_bf(slot).rearrange("p (a b) -> p a b", b=512)
                    for half in range(2):
                        ins = E.dma_start(
                            dst[:, half * 8:(half + 1) * 8, :],
                            wsrc[:, half * 8:(half + 1) * 8, q * 512:(q + 1) * 512])
                        c.tick(ins, f"w{q % 2}", 16,
                               f"{tag}_w{q}" if half == 1 else None)
                if en == "pe":
                    c.wait(E, f"w{q % 2}", f"{tag}_w{q}")
                    for otl in range(4):
                        ot = q * 4 + otl
                        ins = None
                        for dc in range(ND):
                            lhsT = cslot_bf(slot)[:, dc * 512 + otl * 128: dc * 512 + (otl + 1) * 128]
                            for tc in range(4):
                                if ot == 0 and dc == 0:
                                    c.wait(E, "dv", f"s{sub}_hTe{16 * tc + 15}")
                                if ot == 1 and dc == 0 and tc == 0:
                                    c.wait(E, "ac", "gains_bc")
                                if dc == 0 and ot >= 2:
                                    c.wait(E, "ac", f"{tag}_relu{ot-2}_{tc}")
                                ins = E.matmul(up_ring(ot, tc),
                                               lhsT, hT_rhs(dc, tc),
                                               start=(dc == 0), stop=(dc == ND - 1))
                        c.tick(ins, "pe", 1, f"{tag}_mm{ot}")
                    c.tick(None, "pe", 0, f"{tag}_q{q}done")
                if en == "ac":
                    for otl in range(4):
                        ot = q * 4 + otl
                        c.wait(E, "pe", f"{tag}_mm{ot}")
                        for tc in range(4):
                            ins = E.activation(
                                B_bf[:, ot * 2048 + tc * 512: ot * 2048 + (tc + 1) * 512],
                                up_ring(ot, tc), AF.Relu)
                            c.tick(ins, "ac", 1, f"{tag}_relu{ot}_{tc}")
            if en == "ac":
                c.tick(None, "ac", 0, f"{tag}_done")

        def ph_down(sub, en, E):
            tag = f"s{sub}d"
            wd = dram["wdownT"].rearrange("(uc p) o -> p uc o", p=128)
            we = dram["wexp"].rearrange("(dc p) o -> p dc o", p=128)
            scale = 0.5
            def _dn_loads(E, j):
                sl_ = ((j + 1) % 2) * 2
                if j >= 2:
                    c.wait(E, "pe", f"{tag}_j{j-2}done")
                if sl_ == 0:
                    c.wait(E, "pe", f"s{sub}u_q3done")
                if sub == 0 and sl_ == 2 and j == 0:
                    c.wait(E, "ac", "s0_happ3")
                if sub == 2 and sl_ == 2 and j == 0:
                    c.wait(E, "pe", "a5_done")
                for io, (nm, srcv) in enumerate((("d", wd), ("e", we))):
                    dst = cslot_bf(sl_ + io).rearrange("p (a b) -> p a b", b=512)
                    ins = E.dma_start(dst, srcv[:, :, j * 512:(j + 1) * 512])
                    c.tick(ins, f"w{sl_ + io}", 16, f"{tag}_w{nm}{j}")

            if en == "sp" and sub == 0:
                c.wait(E, "pe", f"{tag}_j2done")
                for tt in range(4):
                    for j4 in range(4):
                        kk = j4 * NT + tt
                        pr = kk - kk % 2
                        c.wait(E, f"st{(pr // 2) % 2}", f"{tag}_stk{pr}")
                    ins = E.dma_start(C32[:, 8192 + tt * 2048: 8192 + (tt + 1) * 2048],
                                      dram["y"][tt * 128:(tt + 1) * 128, :])
                    c.tick(ins, f"xp{tt}", 16, f"s1_x{tt}")
            for j in range(4):
                sl = ((j + 1) % 2) * 2
                if en == "gp" and j == 0:
                    _dn_loads(E, 0)
                    _dn_loads(E, 1)
                for tt in range(NT):
                    k = j * NT + tt
                    r = k % 4
                    if en == "pe":
                        if tt == 0:
                            c.wait(E, f"w{sl}", f"{tag}_wd{j}")
                            c.wait(E, f"w{sl+1}", f"{tag}_we{j}")
                            if j == 0:
                                c.wait(E, "ac", f"s{sub}u_done")
                                c.wait(E, "ac", "gains_bc")
                        if k >= 4:
                            c.wait(E, "dv", f"{tag}_ev{k-4}")
                        ins = None
                        for uc in range(ND):
                            ins = E.matmul(ring_ap(r), uT_lhsT(uc, tt),
                                           cslot_bf(sl)[:, uc * 512:(uc + 1) * 512],
                                           start=(uc == 0), stop=False)
                        for dc in range(ND):
                            ins = E.matmul(ring_ap(r), hT_lhsT(dc, tt),
                                           cslot_bf(sl + 1)[:, dc * 512:(dc + 1) * 512],
                                           start=False, stop=(dc == ND - 1))
                        c.tick(ins, "pe", 1, f"{tag}_mm{j}_{tt}")
                        if tt == NT - 1:
                            c.tick(None, "pe", 0, f"{tag}_j{j}done")
                    if en == "dv":
                        c.wait(E, "pe", f"{tag}_mm{j}_{tt}")
                        if k >= 4:
                            pk = ((k - 4) // 2) * 2
                            c.wait(E, f"st{(pk // 2) % 2}", f"{tag}_stk{pk}")
                        dslot = S32[:, (k % 4) * 512:(k % 4) * 512 + 512]
                        ins = E.tensor_scalar(dslot, ring_ap(r), scale, None,
                                              ALU.mult)
                        c.tick(ins, "dv", 1, f"{tag}_ev{k}")
                    if en == "gp":
                        if tt == 2 and j >= 1 and j < 3:
                            _dn_loads(E, j + 1)
                        if tt % 2 == 1:
                            if sub == 0 and j == 0 and tt == 1:
                                c.wait(E, "ini", "xy_copy3")
                            c.wait(E, "dv", f"{tag}_ev{k}")
                            base = ((k - 1) % 4) * 512
                            src = S32[:, base:base + 1024].rearrange(
                                "p (a b) -> p a b", b=512)
                            ins = E.dma_start(
                                y_view[:, tt - 1: tt + 1, j * 512:(j + 1) * 512],
                                src, accum_op=ALU.add)
                            c.tick(ins, f"st{((k - 1) // 2) % 2}", 16, f"{tag}_stk{k-1}")
            if en == "gp":
                c.tick(None, "st0", 0, f"s{sub}_acc_done")
                c.tick(None, "st1", 0, f"s{sub}_acc_done2")

        def ph_kv(en, E):
            tag = "a2"
            wksrc = dram["wk"].rearrange("(dc p) o -> p dc o", p=128)
            wvsrc = dram["wv"].rearrange("(dc p) o -> p dc o", p=128)
            if en == "gp":
                c.wait(E, "pe", "s0d_j3done")
                for nm, srcv, kk in (("wk", wksrc, 0), ("wv", wvsrc, 1)):
                    dst = cslot_bf(kk).rearrange("p (a b) -> p a b", b=512)
                    ins = E.dma_start(dst, srcv)
                    c.tick(ins, f"w{kk}", 16, f"{tag}_{nm}")

            def krm_ap(tt):
                return S32[:, (tt % 2) * 512:(tt % 2) * 512 + 512]

            def kf_ap(tt):
                b0 = 3072 + (tt % 2) * 512
                return S_bf[:, b0: b0 + 512]

            def emit_kT(E, tt):
                c.wait(E, "dv", f"{tag}_rope{tt}")
                if tt >= 2:
                    c.wait(E, "ac", f"{tag}_kTe{tt-2}")
                ins = None
                for kv in range(KVH):
                    ins = E.transpose(
                        pst_ap(tt % 2)[:, kv * 128:(kv + 1) * 128],
                        kf_ap(tt)[:, kv * 128:(kv + 1) * 128],
                        identbf.ap()[:])
                c.tick(ins, "pe", 1, f"{tag}_kT{tt}")

            def emit_kTe(E, tt):
                c.wait(E, "pe", f"{tag}_kT{tt}")
                dst = C_bf[:, 16384:24576].rearrange("p (a b) -> p a b", b=2048)[
                    :, :, tt * 128:(tt + 1) * 128]
                ins = E.activation(dst,
                                   pst_ap(tt % 2)[:, 0:512].rearrange(
                                       "p (a b) -> p a b", b=128), AF.Identity)
                c.tick(ins, "ac", 1, f"{tag}_kTe{tt}")

            for tt in range(NT):
                g2 = tt // 2
                psk_ap = ps_a.ap()[:] if tt % 2 == 0 else ps_big.ap()[:, 0:512]
                psv_ap = ps_b.ap()[:] if tt % 2 == 0 else ps_big.ap()[:, 512:1024]
                sb = 16 + (tt % 2) * 8
                if en == "sp" and tt % 2 == 0:
                    if tt >= 2:
                        c.wait(E, "dv", f"{tag}_rope{tt-1}")
                    for nm in ("cos", "sin"):
                        base = 1024 if nm == "cos" else 1280
                        ins = E.dma_start(
                            S32[:, base:base + 256].rearrange("p (a b) -> p a b", b=128),
                            dram[nm].rearrange("(a p) d -> p a d", p=128)[
                                :, g2 * 2:(g2 + 1) * 2, :])
                        c.tick(ins, "scs", 16,
                               f"{tag}_cs{g2}" if nm == "sin" else None)
                if en == "pe":
                    if tt == 0:
                        c.wait(E, "w0", f"{tag}_wk")
                        c.wait(E, "w1", f"{tag}_wv")
                        c.wait(E, "ac", "gains_bc")
                    c.wait(E, "dv", f"s1_hTe{4 * tt + 3}")
                    if tt >= 2:
                        c.wait(E, "ac", f"{tag}_rms{tt-2}")
                        c.wait(E, "ac", f"{tag}_vev{tt-2}")
                    ins = None
                    for dc in range(ND):
                        E.matmul(psk_ap, hT_lhsT(dc, tt),
                                 cslot_bf(0)[:, dc * 512:(dc + 1) * 512],
                                 start=(dc == 0), stop=(dc == ND - 1))
                        ins = E.matmul(psv_ap, hT_lhsT(dc, tt),
                                       cslot_bf(1)[:, dc * 512:(dc + 1) * 512],
                                       start=(dc == 0), stop=(dc == ND - 1))
                    c.tick(ins, "pe", 1, f"{tag}_mm{tt}")
                    if tt >= 1:
                        emit_kT(E, tt - 1)
                if en == "ac":
                    c.wait(E, "pe", f"{tag}_mm{tt}")
                    ins = E.activation(C_bf[:, 24576 + tt * 512: 24576 + (tt + 1) * 512],
                                        psv_ap, AF.Identity)
                    c.tick(ins, "ac", 1, f"{tag}_vev{tt}")
                    if tt >= 2:
                        c.wait(E, "dv", f"{tag}_rope{tt-2}")
                    for kv in range(KVH):
                        E.activation(krm_ap(tt)[:, kv * 128:(kv + 1) * 128],
                                     psk_ap[:, kv * 128:(kv + 1) * 128],
                                     AF.Square, accum_out=col(stt, sb + kv))
                    E.drain()
                    E.activation(col(stt, sb + 4, 4), col(stt, sb, 4), AF.Ln,
                                 scale=1.0 / 128.0, bias=col(epsc, 1))
                    E.drain()
                    E.activation(col(stt, sb, 4), col(stt, sb + 4, 4), AF.Exp,
                                 scale=-0.5, bias=col(epsc, 2))
                    E.drain()
                    ins = None
                    for kv in range(KVH):
                        ins = E.activation(krm_ap(tt)[:, kv * 128:(kv + 1) * 128],
                                           psk_ap[:, kv * 128:(kv + 1) * 128],
                                           AF.Identity, scale=col(stt, sb + kv))
                    c.tick(ins, "ac", 1, f"{tag}_rms{tt}")
                    if tt >= 1:
                        emit_kTe(E, tt - 1)
                if en == "dv":
                    c.wait(E, "ac", f"{tag}_rms{tt}")
                    c.wait(E, "ac", "gains_bc")
                    krm = krm_ap(tt).rearrange("p (h d) -> p h d", d=128)
                    E.tensor_mul(krm, krm,
                                 kn_bc.ap()[:, None, :].broadcast_to([128, KVH, 128]))
                    E.drain()
                    c.wait(E, "scs", f"{tag}_cs{g2}")
                    if tt >= 2:
                        c.wait(E, "pe", f"{tag}_kT{tt-2}")
                    cosr = S32[:, 1024 + (tt % 2) * 128: 1024 + (tt % 2 + 1) * 128][
                        :, None, :].broadcast_to([128, KVH, 128])
                    sinr = S32[:, 1280 + (tt % 2) * 128: 1280 + (tt % 2 + 1) * 128][
                        :, None, :].broadcast_to([128, KVH, 128])
                    kf = kf_ap(tt).rearrange("p (h d) -> p h d", d=128)
                    ju = junkA.ap()[:, 0:512].rearrange("p (h d) -> p h d", d=128)
                    E.tensor_mul(kf[:, :, 0:64], krm[:, :, 64:128], sinr[:, :, 0:64])
                    E.tensor_mul(kf[:, :, 64:128], krm[:, :, 0:64], sinr[:, :, 64:128])
                    E.tensor_mul(ju, krm, cosr)
                    E.drain()
                    E.tensor_sub(kf[:, :, 0:64], ju[:, :, 0:64], kf[:, :, 0:64])
                    E.tensor_add(kf[:, :, 64:128], ju[:, :, 64:128], kf[:, :, 64:128])
                    c.tickd(E, "dv", f"{tag}_rope{tt}")
            if en == "pe":
                emit_kT(E, NT - 1)
            if en == "ac":
                emit_kTe(E, NT - 1)

        def ph_q(en, E):
            tag = "a4"
            wqsrc = dram["wq"].rearrange("(dc p) o -> p dc o", p=128)

            def qrm_ap(k):
                return S32[:, (k % 2) * 512:(k % 2) * 512 + 512]

            def qf_ap(k):
                b0 = 3072 + (k % 2) * 512
                return S_bf[:, b0: b0 + 512]

            def emit_qT(E, k):
                pq, ptt = divmod(k, NT)
                c.wait(E, "dv", f"{tag}_rope{pq}_{ptt}")
                if k >= 2:
                    p2q, p2t = divmod(k - 2, NT)
                    c.wait(E, "ac", f"{tag}_qTe{p2q}_{p2t}")
                ins = None
                for hj in range(4):
                    ins = E.transpose(
                        pst_ap(k % 2)[:, hj * 128:(hj + 1) * 128],
                        qf_ap(k)[:, hj * 128:(hj + 1) * 128],
                        identbf.ap()[:])
                c.tick(ins, "pe", 1, f"{tag}_qT{pq}_{ptt}")

            def emit_qTe(E, k):
                pq, ptt = divmod(k, NT)
                c.wait(E, "pe", f"{tag}_qT{pq}_{ptt}")
                dst = B_bf.rearrange("p (a b) -> p a b", b=2048)[
                    :, pq * 4:(pq + 1) * 4, ptt * 128:(ptt + 1) * 128]
                ins = E.activation(dst,
                                   pst_ap(k % 2)[:, 0:512].rearrange(
                                       "p (a b) -> p a b", b=128), AF.Identity)
                c.tick(ins, "ac", 1, f"{tag}_qTe{pq}_{ptt}")

            for qq in range(4):
                slot = qq % 2
                if en == "gp":
                    if qq < 2:
                        c.wait(E, "pe", "a2_mm15")
                    else:
                        c.wait(E, "pe", f"{tag}_mm{qq-2}_15")
                    dst = cslot_bf(slot).rearrange("p (a b) -> p a b", b=512)
                    ins = E.dma_start(dst, wqsrc[:, :, qq * 512:(qq + 1) * 512])
                    c.tick(ins, f"w{qq % 2}", 16, f"{tag}_w{qq}")
                for tt in range(NT):
                    g2 = tt // 2
                    k = qq * NT + tt
                    r = k % 4
                    sb = 32 + (tt % 2) * 8
                    if en == "sp" and tt % 2 == 0:
                        if k > 0:
                            pq, ptt = divmod(k - 1, NT)
                            c.wait(E, "dv", f"{tag}_rope{pq}_{ptt}")
                        else:
                            c.wait(E, "dv", "a2_rope15")
                        for nm in ("cos", "sin"):
                            base = 1024 if nm == "cos" else 1280
                            ins = E.dma_start(
                                S32[:, base:base + 256].rearrange("p (a b) -> p a b", b=128),
                                dram[nm].rearrange("(a p) d -> p a d", p=128)[
                                    :, g2 * 2:(g2 + 1) * 2, :])
                            c.tick(ins, "scs", 16,
                                   f"{tag}_cs{qq}_{g2}" if nm == "sin" else None)
                    if en == "pe":
                        if tt == 0:
                            c.wait(E, "w" + str(qq % 2), f"{tag}_w{qq}")
                            if qq == 0:
                                c.wait(E, "ac", "a2_rms15")
                        if k >= 4:
                            pq, ptt = divmod(k - 4, NT)
                            c.wait(E, "ac", f"{tag}_rms{pq}_{ptt}")
                        ins = None
                        for dc in range(ND):
                            ins = E.matmul(ring_ap(r), hT_lhsT(dc, tt),
                                           cslot_bf(slot)[:, dc * 512:(dc + 1) * 512],
                                           start=(dc == 0), stop=(dc == ND - 1))
                        c.tick(ins, "pe", 1, f"{tag}_mm{qq}_{tt}")
                        if k >= 2:
                            emit_qT(E, k - 2)
                    if en == "ac":
                        c.wait(E, "pe", f"{tag}_mm{qq}_{tt}")
                        if k >= 2:
                            pq, ptt = divmod(k - 2, NT)
                            c.wait(E, "dv", f"{tag}_rope{pq}_{ptt}")
                        else:
                            c.wait(E, "dv", "a2_rope15")
                        for hj in range(4):
                            E.activation(qrm_ap(k)[:, hj * 128:(hj + 1) * 128],
                                         ring_ap(r)[:, hj * 128:(hj + 1) * 128],
                                         AF.Square, accum_out=col(stt, sb + hj))
                        E.drain()
                        E.activation(col(stt, sb + 4, 4), col(stt, sb, 4), AF.Ln,
                                     scale=1.0 / 128.0, bias=col(epsc, 1))
                        E.drain()
                        E.activation(col(stt, sb, 4), col(stt, sb + 4, 4), AF.Exp,
                                     scale=-0.5, bias=col(epsc, 2))
                        E.drain()
                        ins = None
                        for hj in range(4):
                            ins = E.activation(qrm_ap(k)[:, hj * 128:(hj + 1) * 128],
                                               ring_ap(r)[:, hj * 128:(hj + 1) * 128],
                                               AF.Identity, scale=col(stt, sb + hj))
                        c.tick(ins, "ac", 1, f"{tag}_rms{qq}_{tt}")
                        if k >= 2:
                            emit_qTe(E, k - 2)
                    if en == "dv":
                        c.wait(E, "ac", f"{tag}_rms{qq}_{tt}")
                        qrm = qrm_ap(k).rearrange("p (h d) -> p h d", d=128)
                        E.tensor_mul(qrm, qrm,
                                     qn_bc.ap()[:, None, :].broadcast_to([128, 4, 128]))
                        E.drain()
                        c.wait(E, "scs", f"{tag}_cs{qq}_{g2}")
                        if k >= 2:
                            pq, ptt = divmod(k - 2, NT)
                            c.wait(E, "pe", f"{tag}_qT{pq}_{ptt}")
                        else:
                            c.wait(E, "pe", "a2_kT15")
                        cosr = S32[:, 1024 + (tt % 2) * 128: 1024 + (tt % 2 + 1) * 128][
                            :, None, :].broadcast_to([128, 4, 128])
                        sinr = S32[:, 1280 + (tt % 2) * 128: 1280 + (tt % 2 + 1) * 128][
                            :, None, :].broadcast_to([128, 4, 128])
                        qf = qf_ap(k).rearrange("p (h d) -> p h d", d=128)
                        ju = junkA.ap()[:, 0:512].rearrange("p (h d) -> p h d", d=128)
                        E.tensor_mul(qf[:, :, 0:64], qrm[:, :, 64:128], sinr[:, :, 0:64])
                        E.tensor_mul(qf[:, :, 64:128], qrm[:, :, 0:64], sinr[:, :, 64:128])
                        E.tensor_mul(ju, qrm, cosr)
                        E.drain()
                        E.tensor_sub(qf[:, :, 0:64], ju[:, :, 0:64], qf[:, :, 0:64])
                        E.tensor_add(qf[:, :, 64:128], ju[:, :, 64:128], qf[:, :, 64:128])
                        c.tickd(E, "dv", f"{tag}_rope{qq}_{tt}")
            if en == "pe":
                emit_qT(E, 62)
                emit_qT(E, 63)
            if en == "ac":
                emit_qTe(E, 62)
                emit_qTe(E, 63)
                c.tick(None, "ac", 0, "a4_qq3done")

        def ph_core(en, E):
            tag = "a5"
            NJ = 64
            den_slot = pstt32[0:1, 512:1024]

            def sT_bank(kt):
                return ps_big.ap()[:, (kt % 4) * 512:(kt % 4) * 512 + 512]

            def pv_bank(j):
                return (ps_a.ap()[:] if j % 2 == 0 else pstt32[:, 0:512])

            def kT_tile(j, kt):
                b0 = 16384 + (j // 16) * 2048 + kt * 128
                return C_bf[:, b0: b0 + 128]

            def qT_rhs(j):
                hh, qc = divmod(j, 4)
                return B_bf[:, hh * 2048 + qc * 512: hh * 2048 + (qc + 1) * 512]

            def p_tile(j, kt):
                b0 = (j % 2) * 8192 + kt * 512
                return C_bf[:, b0: b0 + 512]

            def emit_R(E, j):
                if j >= 1:
                    c.wait(E, "dv", f"{tag}_norm{j-1}")
                c.wait(E, "dv", f"{tag}_recip{j}")
                E.matmul(ps_b.ap()[:], ones1b.ap()[:],
                         S_bf[0:1, 3072:3584], start=True, stop=False)
                ins = E.matmul(ps_b.ap()[:], ones1b.ap()[:],
                               S_bf[0:1, 3584:4096], start=False, stop=True)
                c.tick(ins, "pe", 1, f"{tag}_R{j}")

            def emit_den(E, j):
                ins = None
                for kt in range(16):
                    if kt == 0:
                        c.wait(E, "ac", f"{tag}_exp{j}_7")
                        if j >= 1:
                            c.wait(E, "dv", f"{tag}_recip{j-1}")
                    ins = E.matmul(den_slot, onescol.ap()[:, 0:1], p_tile(j, kt),
                                   start=(kt == 0), stop=(kt == 15))
                c.tick(ins, "pe", 1, f"{tag}_den{j}")

            def emit_pv(E, j, kt):
                if kt == 0 and j >= 2:
                    c.wait(E, "dv", f"{tag}_norm{j-2}")
                ins = E.matmul(pv_bank(j), vt_lhsT(kt, j // 16),
                               p_tile(j, kt), start=(kt == 0), stop=(kt == 15))
                if kt == 15:
                    c.tick(ins, "pe", 1, f"{tag}_pv{j}")

            def emit_recip(E, j):
                c.wait(E, "pe", f"{tag}_den{j}")
                E.tensor_scalar(S32[0:1, 0:512], den_slot, SINK, None, ALU.add)
                E.drain()
                E.reciprocal(S32[0:1, 512:1024], S32[0:1, 0:512])
                E.drain()
                E.tensor_copy(S_bf[0:1, 3072:3584], S32[0:1, 512:1024])
                E.drain()
                ins = E.tensor_sub(S_bf[0:1, 3584:4096], S32[0:1, 512:1024],
                                   S_bf[0:1, 3072:3584])
                c.tick(ins, "dv", 1, f"{tag}_recip{j}")

            def emit_norm(E, j):
                c.wait(E, "pe", f"{tag}_R{j}")
                E.tensor_copy(S32[:, 1024:1536], ps_b.ap()[:])
                E.drain()
                hh, qc = divmod(j, 4)
                dst = A_bf[:, hh * 2048 + qc * 512: hh * 2048 + (qc + 1) * 512]
                ins = E.scalar_tensor_tensor(dst, pv_bank(j), 1.0,
                                             S32[:, 1024:1536], ALU.mult, ALU.mult)
                c.tick(ins, "dv", 1, f"{tag}_norm{j}")

            if en == "pe":
                for j in range(NJ):
                    if j >= 2:
                        emit_R(E, j - 2)
                    if j >= 1:
                        emit_den(E, j - 1)
                    for kt in range(16):
                        if kt >= 4:
                            c.wait(E, "ac", f"{tag}_exp{j}_{(kt - 4) // 2}")
                        if j == 0 and kt == 0:
                            c.wait(E, "ac", "a4_qq3done")
                            c.wait(E, "ac", "a4_rms3_15")
                        ins = E.matmul(sT_bank(kt), kT_tile(j, kt), qT_rhs(j),
                                       start=True, stop=True)
                        c.tick(ins, "pe", 1, f"{tag}_sc{j}_{kt}")
                        if j >= 1:
                            emit_pv(E, j - 1, kt)
                emit_R(E, 62)
                emit_den(E, 63)
                for kt in range(16):
                    emit_pv(E, 63, kt)
                emit_R(E, 63)
                c.tick(None, "pe", 0, "a5_done")
            if en == "ac":
                for j in range(NJ):
                    for p in range(8):
                        if p == 0:
                            if j >= 2:
                                c.wait(E, "pe", f"{tag}_pv{j-2}")
                            if j == 0:
                                c.wait(E, "pe", "a4_mm2_15")
                            if j == 1:
                                c.wait(E, "pe", "a4_mm3_15")
                        c.wait(E, "pe", f"{tag}_sc{j}_{2 * p + 1}")
                        ins = E.activation(
                            C_bf[:, (j % 2) * 8192 + p * 1024:
                                 (j % 2) * 8192 + (p + 1) * 1024],
                            ps_big.ap()[:, (p % 2) * 1024:(p % 2) * 1024 + 1024],
                            AF.Exp)
                        c.tick(ins, "ac", 1, f"{tag}_exp{j}_{p}")
            if en == "dv":
                for j in range(NJ):
                    if j >= 2:
                        emit_norm(E, j - 2)
                    if j >= 1:
                        emit_recip(E, j - 1)
                emit_norm(E, 62)
                emit_recip(E, 63)
                emit_norm(E, 63)

        def ph_wo(en, E):
            tag = "a6"
            wosrc = dram["wo"].rearrange("(hc p) o -> p hc o", p=128)
            def _wo_load(E, j):
                if j < 2:
                    c.wait(E, "pe", "a5_sc63_15")
                    dst = B_bf[:, j * 8192:(j + 1) * 8192].rearrange(
                        "p (a b) -> p a b", b=512)
                else:
                    c.wait(E, "pe", f"{tag}_mm{j-2}_15")
                    c.wait(E, "pe", "a5_done")
                    dst = cslot_bf(j % 2).rearrange("p (a b) -> p a b", b=512)
                ins = E.dma_start(dst, wosrc[:, :, j * 512:(j + 1) * 512])
                c.tick(ins, f"w{j % 2}", 16, f"{tag}_w{j}")

            for j in range(4):
                slot = j % 2
                if en == "gp" and j == 0:
                    _wo_load(E, 0)
                    _wo_load(E, 1)
                for tt in range(NT):
                    k = j * NT + tt
                    r = k % 4
                    if en == "pe":
                        if tt == 0:
                            c.wait(E, f"w{j % 2}", f"{tag}_w{j}")
                            if j == 0:
                                c.wait(E, "dv", "a5_norm63")
                        if k >= 4:
                            c.wait(E, "dv", f"{tag}_ev{k-4}")
                        wsrc_bf = (B_bf[:, j * 8192:(j + 1) * 8192] if j < 2
                                   else cslot_bf(slot))
                        ins = None
                        for hc in range(ND):
                            ins = E.matmul(
                                ring_ap(r), AT_lhsT(hc, tt),
                                wsrc_bf[:, hc * 512:(hc + 1) * 512],
                                start=(hc == 0), stop=(hc == ND - 1))
                        c.tick(ins, "pe", 1, f"{tag}_mm{j}_{tt}")
                    if en == "dv":
                        c.wait(E, "pe", f"{tag}_mm{j}_{tt}")
                        if k >= 4:
                            pk = ((k - 4) // 2) * 2
                            c.wait(E, f"st{(pk // 2) % 2}", f"{tag}_stk{pk}")
                        dslot = S32[:, (k % 4) * 512:(k % 4) * 512 + 512]
                        ins = E.tensor_copy(dslot, ring_ap(r))
                        c.tick(ins, "dv", 1, f"{tag}_ev{k}")
                    if en == "gp":
                        if tt == 2 and j >= 1 and j < 3:
                            _wo_load(E, j + 1)
                        if tt % 2 == 1:
                            c.wait(E, "dv", f"{tag}_ev{k}")
                            base = ((k - 1) % 4) * 512
                            src = S32[:, base:base + 1024].rearrange(
                                "p (a b) -> p a b", b=512)
                            ins = E.dma_start(
                                y_view[:, tt - 1: tt + 1, j * 512:(j + 1) * 512],
                                src, accum_op=ALU.add)
                            c.tick(ins, f"st{((k - 1) // 2) % 2}", 16, f"{tag}_stk{k-1}")
            if en == "gp":
                c.tick(None, "st0", 0, "s1_acc_done")
                c.tick(None, "st1", 0, "s1_acc_done2")

        def ph_fin(en, E):
            if c.mode != "emit":
                return
            if en == "gp":
                for s in ("st0", "st1", "w0", "w1", "w2", "w3"):
                    E.wait_ge(c.handles[s], c.cnt[s])
            if en == "sp":
                for s in ("ini", "xe", "xo", "scs", "xp0", "xp1", "xp2", "xp3"):
                    E.wait_ge(c.handles[s], c.cnt[s])

        def run_all(en, E):
            c.reset()
            ph_init(en, E)
            ph_pre0(en, E)
            ph_hT(0, en, E)
            ph_copy(en, E)
            ph_up(0, en, E)
            ph_down(0, en, E)
            ph_hT(1, en, E)
            ph_kv(en, E)
            ph_q(en, E)
            ph_core(en, E)
            ph_wo(en, E)
            ph_hT(2, en, E)
            ph_up(2, en, E)
            ph_down(2, en, E)
            ph_fin(en, E)

        c.mode = "count"
        for en in ENGS:
            c.reset()
            run_all(en, _Null())
        c.check()

        c.mode = "emit"

        @block.sync
        def _(sync):
            run_all("sp", sync)

        @block.gpsimd
        def _(gpsimd):
            run_all("gp", gpsimd)

        @block.tensor
        def _(tensor):
            run_all("pe", tensor)

        @block.scalar
        def _(scalar):
            run_all("ac", scalar)

        @block.vector
        def _(vector):
            run_all("dv", vector)

    return nc


def kernel(**inputs):
    inp = {k: np.asarray(v) for k, v in inputs.items()}
    hs = np.ascontiguousarray(inp["L0_attn_hidden_states"], dtype=np.float32)
    common = {
        "cos": np.ascontiguousarray(inp["L0_attn_cos"][0, :2048, 0, :], dtype=np.float32),
        "sin": np.ascontiguousarray(inp["L0_attn_sin"][0, :2048, 0, :], dtype=np.float32),
        "wq": np.ascontiguousarray(inp["L0_attn_wq"], dtype=np.float32),
        "wk": np.ascontiguousarray(inp["L0_attn_wk"], dtype=np.float32),
        "wv": np.ascontiguousarray(inp["L0_attn_wv"], dtype=np.float32),
        "wo": np.ascontiguousarray(inp["L0_attn_wo"], dtype=np.float32),
        "wupT": np.ascontiguousarray(np.asarray(inp["L0_ffn_W_up"]).T, dtype=np.float32),
        "wdownT": np.ascontiguousarray(np.asarray(inp["L0_ffn_W_down"]).T, dtype=np.float32),
        "wexp": np.ascontiguousarray(inp["L0_ffn_W_expert"], dtype=np.float32),
        "qn": np.ascontiguousarray(inp["L0_attn_q_norm_w"].reshape(1, 128), dtype=np.float32),
        "kn": np.ascontiguousarray(inp["L0_attn_k_norm_w"].reshape(1, 128), dtype=np.float32),
        "identbf": np.eye(128, dtype=ml_dtypes.bfloat16),
        "ident32": np.eye(128, dtype=np.float32),
    }
    nc = build_nc()
    in_maps = [dict(common, x=np.ascontiguousarray(hs[b])) for b in range(8)]
    res = run_bass_kernel_spmd(nc, in_maps, core_ids=list(range(8)))
    out = np.stack([res.results[b]["y"] for b in range(8)], axis=0)
    return out.astype(np.float32)


if __name__ == "__main__":
    nc = build_nc()
    n = sum(1 for _ in nc.all_instructions())
    print("built ok, instructions:", n)


# version 26
# speedup vs baseline: 1.0453x; 1.0453x over previous
import numpy as np
import ml_dtypes
from contextlib import ExitStack

import concourse.bass as bass
import concourse.mybir as mybir
from concourse.bass_utils import run_bass_kernel_spmd

F32 = mybir.dt.float32
BF16 = mybir.dt.bfloat16
AF = mybir.ActivationFunctionType
ALU = mybir.AluOpType
AX = mybir.AxisListType

NT = 16
ND = 16
DM = 2048
H, KVH, DH = 16, 4, 128
SINK = 640.0
LNC = float(np.log(128.0 ** -0.25))
ENGS = ("sp", "gp", "pe", "ac", "dv")


class _Null:
    def __getattr__(self, k):
        def f(*a, **kw):
            return None
        return f


class Ctx:
    def __init__(self):
        self.mode = "count"
        self.cnt = {}
        self.labels = {}
        self.handles = {}
        self.waited = set()

    def reset(self):
        for k in list(self.cnt):
            self.cnt[k] = 0

    def tick(self, ins, sem, n=1, label=None):
        self.cnt[sem] += n
        if self.mode == "emit" and ins is not None:
            ins.then_inc(self.handles[sem], n)
        if label is not None:
            key = (sem, label)
            if self.mode == "count":
                assert key not in self.labels, f"dup label {key}"
                self.labels[key] = self.cnt[sem]
            else:
                assert self.labels[key] == self.cnt[sem], \
                    f"label drift {key}: {self.labels[key]} vs {self.cnt[sem]}"

    def tickd(self, E, sem, label=None):
        ins = E.drain()
        self.tick(ins, sem, 1, label)

    def wait(self, E, sem, label):
        self.waited.add((sem, label))
        if self.mode == "emit":
            E.wait_ge(self.handles[sem], self.labels[(sem, label)])

    def check(self):
        missing = [k for k in self.waited if k not in self.labels]
        assert not missing, f"missing labels: {sorted(missing)[:8]}"


def build_nc():
    nc = bass.Bass("TRN2", target_bir_lowering=False)

    dram = {}
    dram["x"] = nc.dram_tensor("x", [DM, DM], F32, kind="ExternalInput").ap()
    dram["y"] = nc.dram_tensor("y", [DM, DM], F32, kind="ExternalOutput").ap()
    dram["cos"] = nc.dram_tensor("cos", [DM, DH], F32, kind="ExternalInput").ap()
    dram["sin"] = nc.dram_tensor("sin", [DM, DH], F32, kind="ExternalInput").ap()
    for nm, sh in (("wq", [DM, DM]), ("wk", [DM, 512]), ("wv", [DM, 512]),
                   ("wo", [DM, DM]), ("wupT", [DM, DM]), ("wdownT", [DM, DM]),
                   ("wexp", [DM, DM])):
        dram[nm] = nc.dram_tensor(nm, sh, F32, kind="ExternalInput").ap()
    dram["qn"] = nc.dram_tensor("qn", [1, DH], F32, kind="ExternalInput").ap()
    dram["kn"] = nc.dram_tensor("kn", [1, DH], F32, kind="ExternalInput").ap()
    dram["identbf"] = nc.dram_tensor("identbf", [128, 128], BF16, kind="ExternalInput").ap()
    dram["ident32"] = nc.dram_tensor("ident32", [128, 128], F32, kind="ExternalInput").ap()

    c = Ctx()
    SEMS = ("ini", "xe", "xo", "scs", "w0", "w1", "w2", "w3",
            "st0", "st1", "pe", "ac", "dv", "xp0", "xp1", "xp2", "xp3")
    for s in SEMS:
        c.cnt[s] = 0

    with ExitStack() as stack:
        e = stack.enter_context
        A = e(nc.sbuf_tensor("arenaA", [128, 16384], F32))
        B = e(nc.sbuf_tensor("arenaB", [128, 16384], F32))
        C = e(nc.sbuf_tensor("arenaC", [128, 16384], F32))
        S = e(nc.sbuf_tensor("arenaS", [128, 2080], F32))
        negm_all = e(nc.sbuf_tensor("negm_all", [128, 256], F32))
        qn_bc = e(nc.sbuf_tensor("qn_bc", [128, 128], F32))
        kn_bc = e(nc.sbuf_tensor("kn_bc", [128, 128], F32))
        identbf = e(nc.sbuf_tensor("identbf_sb", [128, 128], BF16))
        ident32 = e(nc.sbuf_tensor("ident32_sb", [128, 128], F32))
        ones = e(nc.sbuf_tensor("ones_sb", [1, 128], F32))
        onescol = e(nc.sbuf_tensor("onescol_sb", [128, 2], BF16))
        ones1b = e(nc.sbuf_tensor("ones1b_sb", [1, 128], BF16))
        epsc = e(nc.sbuf_tensor("epsc", [128, 4], F32))
        stt = e(nc.sbuf_tensor("stats", [128, 64], F32))
        ssbk_all = e(nc.sbuf_tensor("ssbk_all", [128, 64], F32))
        mkh = e(nc.sbuf_tensor("mkh", [128, 4], F32))
        junkA = e(nc.sbuf_tensor("junkA", [128, 512], F32))
        e_all = e(nc.sbuf_tensor("e_all", [128, 256], F32))
        ps_big = e(nc.psum_tensor("ps_big", [128, 2048], F32))
        ps_tt = e(nc.psum_tensor("ps_tt", [128, 2048], BF16))
        ps_a = e(nc.psum_tensor("ps_a", [128, 512], F32))
        ps_b = e(nc.psum_tensor("ps_b", [128, 512], F32))

        pstt32 = ps_tt.ap().bitcast(F32)

        def ring_ap(r):
            if r == 0:
                return ps_a.ap()[:]
            if r == 1:
                return ps_b.ap()[:]
            if r == 2:
                return ps_big.ap()[:, 0:512]
            return ps_big.ap()[:, 512:1024]

        def pst_ap(parity):
            return ps_tt.ap()[:, parity * 1024: parity * 1024 + 1024]
        for s in SEMS:
            c.handles[s] = e(nc.semaphore(f"sem_{s}"))
        block = e(nc.Block())

        A32, B32, C32, S32 = A.ap(), B.ap(), C.ap(), S.ap()
        A_bf = A32.bitcast(BF16)
        B_bf = B32.bitcast(BF16)
        C_bf = C32.bitcast(BF16)
        S_bf = S32.bitcast(BF16)

        def col(t, j, w=1):
            return t.ap()[:, j:j + w]

        qn_rowV = junkA.ap()[0:1, 256:384]
        kn_rowV = junkA.ap()[0:1, 384:512]
        tr64V = junkA.ap()[0:64, 0:128]
        t2cV = junkA.ap()[0:64, 192:193]
        t3rV = junkA.ap()[0:1, 128:192]
        mkn2V = junkA.ap()[0:1, 200:204]

        def hT_lhsT(dc, tt):
            return A_bf[:, dc * 2048 + tt * 128: dc * 2048 + (tt + 1) * 128]

        def hT_rhs(dc, tc):
            return A_bf[:, dc * 2048 + tc * 512: dc * 2048 + (tc + 1) * 512]

        def AT_lhsT(hc, tt):
            return A_bf[:, hc * 2048 + tt * 128: hc * 2048 + (tt + 1) * 128]

        def uT_lhsT(uc, tt):
            return B_bf[:, uc * 2048 + tt * 128: uc * 2048 + (tt + 1) * 128]

        def qT_lhsT(h, qg):
            return B_bf[:, h * 2048 + qg * 128: h * 2048 + (qg + 1) * 128]

        def cslot_bf(k):
            return C_bf[:, k * 8192:(k + 1) * 8192]

        def kT_rhs(kvh, kc):
            b0 = 16384 + kvh * 2048 + kc * 512
            return C_bf[:, b0: b0 + 512]

        def vt_lhsT(kc, kvh):
            b0 = 24576 + kc * 512 + kvh * 128
            return C_bf[:, b0: b0 + 128]

        def p_slot(par):
            return C_bf[:, par * 2048:(par + 1) * 2048]

        def pT_rhs(kc):
            return C_bf[:, 4096 + kc * 512: 4096 + (kc + 1) * 512]

        def xsrc(sub):
            return dram["x"] if sub == 0 else dram["y"]

        y_view = dram["y"].rearrange("(t p) o -> p t o", p=128)

        def ph_init(en, E):
            if en == "sp":
                dsts = (("identbf", identbf.ap()[:]), ("ident32", ident32.ap()[:]),
                        ("qn", qn_rowV), ("kn", kn_rowV))
                for i, (nm, dst) in enumerate(dsts):
                    ins = E.dma_start(dst, dram[nm])
                    c.tick(ins, "ini", 16, "ld_all" if i == 3 else None)
            if en == "dv":
                E.memset(ones.ap()[:], 1.0)
                E.memset(onescol.ap()[:], 1.0)
                E.memset(ones1b.ap()[:], 1.0)
                E.memset(col(epsc, 0), 1e-5)
                E.memset(col(epsc, 1), 1e-6)
                E.memset(col(epsc, 2), LNC)
                c.tickd(E, "dv", "init_consts")
            if en == "pe":
                c.wait(E, "ini", "ld_all")
                c.wait(E, "dv", "init_consts")
                E.matmul(ps_b.ap()[:, 0:128], ones.ap()[:], qn_rowV,
                         start=True, stop=True)
                ins = E.matmul(ps_b.ap()[:, 128:256], ones.ap()[:], kn_rowV,
                               start=True, stop=True)
                c.tick(ins, "pe", 1, "gains_mm")
            if en == "ac":
                c.wait(E, "pe", "gains_mm")
                E.activation(qn_bc.ap()[:], ps_b.ap()[:, 0:128], AF.Identity)
                E.activation(kn_bc.ap()[:], ps_b.ap()[:, 128:256], AF.Identity)
                c.tickd(E, "ac", "gains_bc")

        def ph_pre0(en, E):
            if en == "sp":
                for tt in range(4):
                    ins = E.dma_start(C32[:, 8192 + tt * 2048: 8192 + (tt + 1) * 2048],
                                      dram["x"][tt * 128:(tt + 1) * 128, :])
                    c.tick(ins, f"xp{tt}", 16, f"s0_x{tt}")

        def ph_copy(en, E):
            if en == "sp":
                for i in range(4):
                    ins = E.dma_start(dram["y"][i * 512:(i + 1) * 512, :],
                                      dram["x"][i * 512:(i + 1) * 512, :])
                    c.tick(ins, "ini", 16, "xy_copy3" if i == 3 else None)

        def ph_hT(sub, en, E):
            tag = f"s{sub}"
            src = xsrc(sub)
            npre = 4 if sub in (0, 1) else 0

            def xbuf(tt):
                if tt < npre:
                    return C32[:, 8192 + tt * 2048: 8192 + (tt + 1) * 2048]
                b0 = (0, 2048, 6144, 8192)[tt % 4]
                return B32[:, b0: b0 + 2048]

            def hbuf(tt):
                b0 = 8192 + (tt % 2) * 2048
                return B_bf[:, b0: b0 + 2048]

            for tt in range(NT):
                sb = (tt % 4) * 8
                if en == "sp":
                    if sub == 1 and tt >= npre:
                        c.wait(E, "st0", "s0_acc_done")
                        c.wait(E, "st1", "s0_acc_done2")
                    if sub == 2:
                        for j4 in range(4):
                            kk = j4 * NT + tt
                            pr = kk - kk % 2
                            c.wait(E, f"st{(pr // 2) % 2}", f"a6_stk{pr}")
                    if tt >= npre:
                        if tt >= 4 + npre:
                            c.wait(E, "ac", f"{tag}_happ{tt-4}")
                        ins = E.dma_start(xbuf(tt), src[tt * 128:(tt + 1) * 128, :])
                        c.tick(ins, "xe" if tt % 2 == 0 else "xo", 16, f"{tag}_x{tt}")
                if en == "dv":
                    xs = (f"xp{tt}" if tt < npre else ("xe" if tt % 2 == 0 else "xo"))
                    c.wait(E, xs, f"{tag}_x{tt}")
                    if tt >= 4:
                        c.wait(E, "ac", f"{tag}_happ{tt-4}")
                    for i4 in range(4):
                        E.bn_stats(junkA.ap()[:, i4 * 6:(i4 + 1) * 6],
                                   xbuf(tt)[:, i4 * 512:(i4 + 1) * 512])
                    E.drain()
                    E.bn_aggr(col(stt, sb + 0, 2), junkA.ap()[:, 0:24])
                    c.tickd(E, "dv", f"{tag}_ssum{tt}")
                    if tt >= 1:
                        for g in range(4):
                            gid = (tt - 1) * 4 + g
                            gp2 = gid % 2
                            c.wait(E, "pe", f"{tag}_hT{tt-1}g{g}")
                            dst = A_bf.rearrange("p (a b) -> p a b", b=2048)[
                                :, g * 4:(g + 1) * 4, (tt - 1) * 128:tt * 128]
                            src_ps = pst_ap(gp2)[:, 0:512].rearrange(
                                "p (a b) -> p a b", b=128)
                            ins = E.tensor_copy(dst, src_ps)
                            c.tick(ins, "dv", 1, f"{tag}_hTe{gid}")
                if en == "ac":
                    c.wait(E, "dv", f"{tag}_ssum{tt}")
                    E.activation(col(stt, sb + 7), col(stt, sb + 1), AF.Ln,
                                 bias=col(epsc, 0))
                    E.drain()
                    E.activation(col(stt, sb + 6), col(stt, sb + 7), AF.Exp,
                                 scale=-0.5)
                    E.drain()
                    E.activation(col(stt, sb + 3), col(stt, sb + 0), AF.Identity,
                                 scale=col(stt, sb + 6))
                    E.drain()
                    E.activation(col(stt, sb + 5), col(stt, sb + 3), AF.Identity,
                                 scale=-1.0)
                    E.drain()
                    if tt >= 2:
                        c.wait(E, "pe", f"{tag}_hT{tt-2}g3")
                    for hc4 in range(4):
                        ins = E.activation(hbuf(tt)[:, hc4 * 512:(hc4 + 1) * 512],
                                           xbuf(tt)[:, hc4 * 512:(hc4 + 1) * 512],
                                           AF.Identity,
                                           scale=col(stt, sb + 6), bias=col(stt, sb + 5))
                        c.tick(ins, "ac", 1, f"{tag}_happ{tt}c{hc4}")
                    c.tick(None, "ac", 0, f"{tag}_happ{tt}")
                if en == "pe":
                    for g in range(4):
                        gid = tt * 4 + g
                        gp2 = gid % 2
                        c.wait(E, "ac", f"{tag}_happ{tt}c{g}")
                        if gid >= 2:
                            c.wait(E, "dv", f"{tag}_hTe{gid-2}")
                        ins = None
                        for j in range(4):
                            dc = g * 4 + j
                            ins = E.transpose(
                                pst_ap(gp2)[:, j * 128:(j + 1) * 128],
                                hbuf(tt)[:, dc * 128:(dc + 1) * 128],
                                identbf.ap()[:])
                        c.tick(ins, "pe", 1, f"{tag}_hT{tt}g{g}")
            if en == "dv":
                for g in range(4):
                    gid = (NT - 1) * 4 + g
                    gp2 = gid % 2
                    c.wait(E, "pe", f"{tag}_hT{NT-1}g{g}")
                    dst = A_bf.rearrange("p (a b) -> p a b", b=2048)[
                        :, g * 4:(g + 1) * 4, (NT - 1) * 128:NT * 128]
                    src_ps = pst_ap(gp2)[:, 0:512].rearrange(
                        "p (a b) -> p a b", b=128)
                    ins = E.tensor_copy(dst, src_ps)
                    c.tick(ins, "dv", 1, f"{tag}_hTe{gid}")
                c.tick(None, "dv", 0, f"{tag}_hT_done")

        def ph_up(sub, en, E):
            tag = f"s{sub}u"
            wsrc = dram["wupT"].rearrange("(dc p) o -> p dc o", p=128)

            def up_ring(ot, tc):
                if ot % 2 == 0:
                    return ps_big.ap()[:, tc * 512:(tc + 1) * 512]
                return (ps_a.ap()[:], ps_b.ap()[:],
                        pstt32[:, 0:512], pstt32[:, 512:1024])[tc]
            for q in range(4):
                slot = q % 2
                if en == "gp":
                    if q >= 2:
                        c.wait(E, "pe", f"{tag}_q{q-2}done")
                    if sub == 2 and q < 2:
                        c.wait(E, "pe", "a5_done")
                    dst = cslot_bf(slot).rearrange("p (a b) -> p a b", b=512)
                    for half in range(2):
                        ins = E.dma_start(
                            dst[:, half * 8:(half + 1) * 8, :],
                            wsrc[:, half * 8:(half + 1) * 8, q * 512:(q + 1) * 512])
                        c.tick(ins, f"w{q % 2}", 16,
                               f"{tag}_w{q}" if half == 1 else None)
                if en == "pe":
                    c.wait(E, f"w{q % 2}", f"{tag}_w{q}")
                    for otl in range(4):
                        ot = q * 4 + otl
                        ins = None
                        for dc in range(ND):
                            lhsT = cslot_bf(slot)[:, dc * 512 + otl * 128: dc * 512 + (otl + 1) * 128]
                            for tc in range(4):
                                if ot == 0 and dc == 0:
                                    c.wait(E, "dv", f"s{sub}_hTe{16 * tc + 15}")
                                if ot == 1 and dc == 0 and tc == 0:
                                    c.wait(E, "ac", "gains_bc")
                                if dc == 0 and ot >= 2:
                                    c.wait(E, "ac", f"{tag}_relu{ot-2}_{tc}")
                                ins = E.matmul(up_ring(ot, tc),
                                               lhsT, hT_rhs(dc, tc),
                                               start=(dc == 0), stop=(dc == ND - 1))
                        c.tick(ins, "pe", 1, f"{tag}_mm{ot}")
                    c.tick(None, "pe", 0, f"{tag}_q{q}done")
                if en == "ac":
                    for otl in range(4):
                        ot = q * 4 + otl
                        c.wait(E, "pe", f"{tag}_mm{ot}")
                        for tc in range(4):
                            ins = E.activation(
                                B_bf[:, ot * 2048 + tc * 512: ot * 2048 + (tc + 1) * 512],
                                up_ring(ot, tc), AF.Relu)
                            c.tick(ins, "ac", 1, f"{tag}_relu{ot}_{tc}")
            if en == "ac":
                c.tick(None, "ac", 0, f"{tag}_done")

        def ph_down(sub, en, E):
            tag = f"s{sub}d"
            wd = dram["wdownT"].rearrange("(uc p) o -> p uc o", p=128)
            we = dram["wexp"].rearrange("(dc p) o -> p dc o", p=128)
            scale = 0.5
            def _dn_loads(E, j):
                sl_ = ((j + 1) % 2) * 2
                if j >= 2:
                    c.wait(E, "pe", f"{tag}_j{j-2}done")
                if sl_ == 0:
                    c.wait(E, "pe", f"s{sub}u_q3done")
                if sub == 0 and sl_ == 2 and j == 0:
                    c.wait(E, "ac", "s0_happ3")
                if sub == 2 and sl_ == 2 and j == 0:
                    c.wait(E, "pe", "a5_done")
                for io, (nm, srcv) in enumerate((("d", wd), ("e", we))):
                    dst = cslot_bf(sl_ + io).rearrange("p (a b) -> p a b", b=512)
                    ins = E.dma_start(dst, srcv[:, :, j * 512:(j + 1) * 512])
                    c.tick(ins, f"w{sl_ + io}", 16, f"{tag}_w{nm}{j}")

            if en == "sp" and sub == 0:
                c.wait(E, "pe", f"{tag}_j2done")
                for tt in range(4):
                    for j4 in range(4):
                        kk = j4 * NT + tt
                        pr = kk - kk % 2
                        c.wait(E, f"st{(pr // 2) % 2}", f"{tag}_stk{pr}")
                    ins = E.dma_start(C32[:, 8192 + tt * 2048: 8192 + (tt + 1) * 2048],
                                      dram["y"][tt * 128:(tt + 1) * 128, :])
                    c.tick(ins, f"xp{tt}", 16, f"s1_x{tt}")
            for j in range(4):
                sl = ((j + 1) % 2) * 2
                if en == "gp" and j == 0:
                    _dn_loads(E, 0)
                    _dn_loads(E, 1)
                for tt in range(NT):
                    k = j * NT + tt
                    r = k % 4
                    if en == "pe":
                        if tt == 0:
                            c.wait(E, f"w{sl}", f"{tag}_wd{j}")
                            c.wait(E, f"w{sl+1}", f"{tag}_we{j}")
                            if j == 0:
                                c.wait(E, "ac", f"s{sub}u_done")
                                c.wait(E, "ac", "gains_bc")
                        if k >= 4:
                            c.wait(E, "dv", f"{tag}_ev{k-4}")
                        ins = None
                        for uc in range(ND):
                            ins = E.matmul(ring_ap(r), uT_lhsT(uc, tt),
                                           cslot_bf(sl)[:, uc * 512:(uc + 1) * 512],
                                           start=(uc == 0), stop=False)
                        for dc in range(ND):
                            ins = E.matmul(ring_ap(r), hT_lhsT(dc, tt),
                                           cslot_bf(sl + 1)[:, dc * 512:(dc + 1) * 512],
                                           start=False, stop=(dc == ND - 1))
                        c.tick(ins, "pe", 1, f"{tag}_mm{j}_{tt}")
                        if tt == NT - 1:
                            c.tick(None, "pe", 0, f"{tag}_j{j}done")
                    if en == "dv":
                        c.wait(E, "pe", f"{tag}_mm{j}_{tt}")
                        if k >= 4:
                            pk = ((k - 4) // 2) * 2
                            c.wait(E, f"st{(pk // 2) % 2}", f"{tag}_stk{pk}")
                        dslot = S32[:, (k % 4) * 512:(k % 4) * 512 + 512]
                        ins = E.tensor_scalar(dslot, ring_ap(r), scale, None,
                                              ALU.mult)
                        c.tick(ins, "dv", 1, f"{tag}_ev{k}")
                    if en == "gp":
                        if tt == 2 and j >= 1 and j < 3:
                            _dn_loads(E, j + 1)
                        if tt % 2 == 1:
                            if sub == 0 and j == 0 and tt == 1:
                                c.wait(E, "ini", "xy_copy3")
                            c.wait(E, "dv", f"{tag}_ev{k}")
                            base = ((k - 1) % 4) * 512
                            src = S32[:, base:base + 1024].rearrange(
                                "p (a b) -> p a b", b=512)
                            ins = E.dma_start(
                                y_view[:, tt - 1: tt + 1, j * 512:(j + 1) * 512],
                                src, accum_op=ALU.add)
                            c.tick(ins, f"st{((k - 1) // 2) % 2}", 16, f"{tag}_stk{k-1}")
            if en == "gp":
                c.tick(None, "st0", 0, f"s{sub}_acc_done")
                c.tick(None, "st1", 0, f"s{sub}_acc_done2")

        def ph_kv(en, E):
            tag = "a2"
            wksrc = dram["wk"].rearrange("(dc p) o -> p dc o", p=128)
            wvsrc = dram["wv"].rearrange("(dc p) o -> p dc o", p=128)
            if en == "gp":
                c.wait(E, "pe", "s0d_j3done")
                for nm, srcv, kk in (("wk", wksrc, 0), ("wv", wvsrc, 1)):
                    dst = cslot_bf(kk).rearrange("p (a b) -> p a b", b=512)
                    ins = E.dma_start(dst, srcv)
                    c.tick(ins, f"w{kk}", 16, f"{tag}_{nm}")

            def krm_ap(tt):
                return S32[:, (tt % 2) * 512:(tt % 2) * 512 + 512]

            def kf_ap(tt):
                b0 = 3072 + (tt % 2) * 512
                return S_bf[:, b0: b0 + 512]

            def emit_kT(E, tt):
                c.wait(E, "dv", f"{tag}_rope{tt}")
                if tt >= 2:
                    c.wait(E, "ac", f"{tag}_kTe{tt-2}")
                ins = None
                for kv in range(KVH):
                    ins = E.transpose(
                        pst_ap(tt % 2)[:, kv * 128:(kv + 1) * 128],
                        kf_ap(tt)[:, kv * 128:(kv + 1) * 128],
                        identbf.ap()[:])
                c.tick(ins, "pe", 1, f"{tag}_kT{tt}")

            def emit_kTe(E, tt):
                c.wait(E, "pe", f"{tag}_kT{tt}")
                dst = C_bf[:, 16384:24576].rearrange("p (a b) -> p a b", b=2048)[
                    :, :, tt * 128:(tt + 1) * 128]
                ins = E.activation(dst,
                                   pst_ap(tt % 2)[:, 0:512].rearrange(
                                       "p (a b) -> p a b", b=128), AF.Identity)
                c.tick(ins, "ac", 1, f"{tag}_kTe{tt}")

            for tt in range(NT):
                g2 = tt // 2
                psk_ap = ps_a.ap()[:] if tt % 2 == 0 else ps_big.ap()[:, 0:512]
                psv_ap = ps_b.ap()[:] if tt % 2 == 0 else ps_big.ap()[:, 512:1024]
                sb = 16 + (tt % 2) * 8
                if en == "sp" and tt % 2 == 0:
                    if tt >= 2:
                        c.wait(E, "dv", f"{tag}_rope{tt-1}")
                    for nm in ("cos", "sin"):
                        base = 1024 if nm == "cos" else 1280
                        ins = E.dma_start(
                            S32[:, base:base + 256].rearrange("p (a b) -> p a b", b=128),
                            dram[nm].rearrange("(a p) d -> p a d", p=128)[
                                :, g2 * 2:(g2 + 1) * 2, :])
                        c.tick(ins, "scs", 16,
                               f"{tag}_cs{g2}" if nm == "sin" else None)
                if en == "pe":
                    if tt == 0:
                        c.wait(E, "w0", f"{tag}_wk")
                        c.wait(E, "w1", f"{tag}_wv")
                        c.wait(E, "ac", "gains_bc")
                    c.wait(E, "dv", f"s1_hTe{4 * tt + 3}")
                    if tt >= 2:
                        c.wait(E, "ac", f"{tag}_rms{tt-2}")
                        c.wait(E, "ac", f"{tag}_vev{tt-2}")
                    ins = None
                    for dc in range(ND):
                        E.matmul(psk_ap, hT_lhsT(dc, tt),
                                 cslot_bf(0)[:, dc * 512:(dc + 1) * 512],
                                 start=(dc == 0), stop=(dc == ND - 1))
                        ins = E.matmul(psv_ap, hT_lhsT(dc, tt),
                                       cslot_bf(1)[:, dc * 512:(dc + 1) * 512],
                                       start=(dc == 0), stop=(dc == ND - 1))
                    c.tick(ins, "pe", 1, f"{tag}_mm{tt}")
                    if tt >= 1:
                        emit_kT(E, tt - 1)
                if en == "ac":
                    c.wait(E, "pe", f"{tag}_mm{tt}")
                    ins = E.activation(C_bf[:, 24576 + tt * 512: 24576 + (tt + 1) * 512],
                                        psv_ap, AF.Identity)
                    c.tick(ins, "ac", 1, f"{tag}_vev{tt}")
                    if tt >= 2:
                        c.wait(E, "dv", f"{tag}_rope{tt-2}")
                    for kv in range(KVH):
                        E.activation(krm_ap(tt)[:, kv * 128:(kv + 1) * 128],
                                     psk_ap[:, kv * 128:(kv + 1) * 128],
                                     AF.Square, accum_out=col(stt, sb + kv))
                    E.drain()
                    E.activation(col(stt, sb + 4, 4), col(stt, sb, 4), AF.Ln,
                                 scale=1.0 / 128.0, bias=col(epsc, 1))
                    E.drain()
                    E.activation(col(stt, sb, 4), col(stt, sb + 4, 4), AF.Exp,
                                 scale=-0.5, bias=col(epsc, 2))
                    E.drain()
                    ins = None
                    for kv in range(KVH):
                        ins = E.activation(krm_ap(tt)[:, kv * 128:(kv + 1) * 128],
                                           psk_ap[:, kv * 128:(kv + 1) * 128],
                                           AF.Identity, scale=col(stt, sb + kv))
                    c.tick(ins, "ac", 1, f"{tag}_rms{tt}")
                    if tt >= 1:
                        emit_kTe(E, tt - 1)
                if en == "dv":
                    c.wait(E, "ac", f"{tag}_rms{tt}")
                    c.wait(E, "ac", "gains_bc")
                    krm = krm_ap(tt).rearrange("p (h d) -> p h d", d=128)
                    E.tensor_mul(krm, krm,
                                 kn_bc.ap()[:, None, :].broadcast_to([128, KVH, 128]))
                    E.drain()
                    c.wait(E, "scs", f"{tag}_cs{g2}")
                    if tt >= 2:
                        c.wait(E, "pe", f"{tag}_kT{tt-2}")
                    cosr = S32[:, 1024 + (tt % 2) * 128: 1024 + (tt % 2 + 1) * 128][
                        :, None, :].broadcast_to([128, KVH, 128])
                    sinr = S32[:, 1280 + (tt % 2) * 128: 1280 + (tt % 2 + 1) * 128][
                        :, None, :].broadcast_to([128, KVH, 128])
                    kf = kf_ap(tt).rearrange("p (h d) -> p h d", d=128)
                    ju = junkA.ap()[:, 0:512].rearrange("p (h d) -> p h d", d=128)
                    E.tensor_mul(kf[:, :, 0:64], krm[:, :, 64:128], sinr[:, :, 0:64])
                    E.tensor_mul(kf[:, :, 64:128], krm[:, :, 0:64], sinr[:, :, 64:128])
                    E.tensor_mul(ju, krm, cosr)
                    E.drain()
                    E.tensor_sub(kf[:, :, 0:64], ju[:, :, 0:64], kf[:, :, 0:64])
                    E.tensor_add(kf[:, :, 64:128], ju[:, :, 64:128], kf[:, :, 64:128])
                    c.tickd(E, "dv", f"{tag}_rope{tt}")
            if en == "pe":
                emit_kT(E, NT - 1)
            if en == "ac":
                emit_kTe(E, NT - 1)

        def ph_q(en, E):
            tag = "a4"
            wqsrc = dram["wq"].rearrange("(dc p) o -> p dc o", p=128)

            def qrm_ap(k):
                return S32[:, (k % 2) * 512:(k % 2) * 512 + 512]

            def qf_ap(k):
                b0 = 3072 + (k % 2) * 512
                return S_bf[:, b0: b0 + 512]

            def emit_qT(E, k):
                pq, ptt = divmod(k, NT)
                c.wait(E, "dv", f"{tag}_rope{pq}_{ptt}")
                if k >= 2:
                    p2q, p2t = divmod(k - 2, NT)
                    c.wait(E, "ac", f"{tag}_qTe{p2q}_{p2t}")
                ins = None
                for hj in range(4):
                    ins = E.transpose(
                        pst_ap(k % 2)[:, hj * 128:(hj + 1) * 128],
                        qf_ap(k)[:, hj * 128:(hj + 1) * 128],
                        identbf.ap()[:])
                c.tick(ins, "pe", 1, f"{tag}_qT{pq}_{ptt}")

            def emit_qTe(E, k):
                pq, ptt = divmod(k, NT)
                c.wait(E, "pe", f"{tag}_qT{pq}_{ptt}")
                dst = B_bf.rearrange("p (a b) -> p a b", b=2048)[
                    :, pq * 4:(pq + 1) * 4, ptt * 128:(ptt + 1) * 128]
                ins = E.activation(dst,
                                   pst_ap(k % 2)[:, 0:512].rearrange(
                                       "p (a b) -> p a b", b=128), AF.Identity)
                c.tick(ins, "ac", 1, f"{tag}_qTe{pq}_{ptt}")

            for qq in range(4):
                slot = qq % 2
                if en == "gp":
                    if qq < 2:
                        c.wait(E, "pe", "a2_mm15")
                    else:
                        c.wait(E, "pe", f"{tag}_mm{qq-2}_15")
                    dst = cslot_bf(slot).rearrange("p (a b) -> p a b", b=512)
                    ins = E.dma_start(dst, wqsrc[:, :, qq * 512:(qq + 1) * 512])
                    c.tick(ins, f"w{qq % 2}", 16, f"{tag}_w{qq}")
                for tt in range(NT):
                    g2 = tt // 2
                    k = qq * NT + tt
                    r = k % 4
                    sb = 32 + (tt % 2) * 8
                    if en == "sp" and tt % 2 == 0:
                        if k > 0:
                            pq, ptt = divmod(k - 1, NT)
                            c.wait(E, "dv", f"{tag}_rope{pq}_{ptt}")
                        else:
                            c.wait(E, "dv", "a2_rope15")
                        for nm in ("cos", "sin"):
                            base = 1024 if nm == "cos" else 1280
                            ins = E.dma_start(
                                S32[:, base:base + 256].rearrange("p (a b) -> p a b", b=128),
                                dram[nm].rearrange("(a p) d -> p a d", p=128)[
                                    :, g2 * 2:(g2 + 1) * 2, :])
                            c.tick(ins, "scs", 16,
                                   f"{tag}_cs{qq}_{g2}" if nm == "sin" else None)
                    if en == "pe":
                        if tt == 0:
                            c.wait(E, "w" + str(qq % 2), f"{tag}_w{qq}")
                            if qq == 0:
                                c.wait(E, "ac", "a2_rms15")
                        if k >= 4:
                            pq, ptt = divmod(k - 4, NT)
                            c.wait(E, "ac", f"{tag}_rms{pq}_{ptt}")
                        ins = None
                        for dc in range(ND):
                            ins = E.matmul(ring_ap(r), hT_lhsT(dc, tt),
                                           cslot_bf(slot)[:, dc * 512:(dc + 1) * 512],
                                           start=(dc == 0), stop=(dc == ND - 1))
                        c.tick(ins, "pe", 1, f"{tag}_mm{qq}_{tt}")
                        if k >= 2:
                            emit_qT(E, k - 2)
                    if en == "ac":
                        c.wait(E, "pe", f"{tag}_mm{qq}_{tt}")
                        if k >= 2:
                            pq, ptt = divmod(k - 2, NT)
                            c.wait(E, "dv", f"{tag}_rope{pq}_{ptt}")
                        else:
                            c.wait(E, "dv", "a2_rope15")
                        for hj in range(4):
                            E.activation(qrm_ap(k)[:, hj * 128:(hj + 1) * 128],
                                         ring_ap(r)[:, hj * 128:(hj + 1) * 128],
                                         AF.Square, accum_out=col(stt, sb + hj))
                        E.drain()
                        E.activation(col(stt, sb + 4, 4), col(stt, sb, 4), AF.Ln,
                                     scale=1.0 / 128.0, bias=col(epsc, 1))
                        E.drain()
                        E.activation(col(stt, sb, 4), col(stt, sb + 4, 4), AF.Exp,
                                     scale=-0.5, bias=col(epsc, 2))
                        E.drain()
                        ins = None
                        for hj in range(4):
                            ins = E.activation(qrm_ap(k)[:, hj * 128:(hj + 1) * 128],
                                               ring_ap(r)[:, hj * 128:(hj + 1) * 128],
                                               AF.Identity, scale=col(stt, sb + hj))
                        c.tick(ins, "ac", 1, f"{tag}_rms{qq}_{tt}")
                        if k >= 2:
                            emit_qTe(E, k - 2)
                    if en == "dv":
                        c.wait(E, "ac", f"{tag}_rms{qq}_{tt}")
                        qrm = qrm_ap(k).rearrange("p (h d) -> p h d", d=128)
                        E.tensor_mul(qrm, qrm,
                                     qn_bc.ap()[:, None, :].broadcast_to([128, 4, 128]))
                        E.drain()
                        c.wait(E, "scs", f"{tag}_cs{qq}_{g2}")
                        if k >= 2:
                            pq, ptt = divmod(k - 2, NT)
                            c.wait(E, "pe", f"{tag}_qT{pq}_{ptt}")
                        else:
                            c.wait(E, "pe", "a2_kT15")
                        cosr = S32[:, 1024 + (tt % 2) * 128: 1024 + (tt % 2 + 1) * 128][
                            :, None, :].broadcast_to([128, 4, 128])
                        sinr = S32[:, 1280 + (tt % 2) * 128: 1280 + (tt % 2 + 1) * 128][
                            :, None, :].broadcast_to([128, 4, 128])
                        qf = qf_ap(k).rearrange("p (h d) -> p h d", d=128)
                        ju = junkA.ap()[:, 0:512].rearrange("p (h d) -> p h d", d=128)
                        E.tensor_mul(qf[:, :, 0:64], qrm[:, :, 64:128], sinr[:, :, 0:64])
                        E.tensor_mul(qf[:, :, 64:128], qrm[:, :, 0:64], sinr[:, :, 64:128])
                        E.tensor_mul(ju, qrm, cosr)
                        E.drain()
                        E.tensor_sub(qf[:, :, 0:64], ju[:, :, 0:64], qf[:, :, 0:64])
                        E.tensor_add(qf[:, :, 64:128], ju[:, :, 64:128], qf[:, :, 64:128])
                        c.tickd(E, "dv", f"{tag}_rope{qq}_{tt}")
            if en == "pe":
                emit_qT(E, 62)
                emit_qT(E, 63)
            if en == "ac":
                emit_qTe(E, 62)
                emit_qTe(E, 63)
                c.tick(None, "ac", 0, "a4_qq3done")

        def ph_core(en, E):
            tag = "a5"
            NJ = 64
            den_slot = pstt32[0:1, 512:1024]

            def sT_bank(kt):
                return ps_big.ap()[:, (kt % 4) * 512:(kt % 4) * 512 + 512]

            def pv_bank(j):
                return (ps_a.ap()[:] if j % 2 == 0 else pstt32[:, 0:512])

            def kT_tile(j, kt):
                b0 = 16384 + (j // 16) * 2048 + kt * 128
                return C_bf[:, b0: b0 + 128]

            def qT_rhs(j):
                hh, qc = divmod(j, 4)
                return B_bf[:, hh * 2048 + qc * 512: hh * 2048 + (qc + 1) * 512]

            def p_tile(j, kt):
                b0 = (j % 2) * 8192 + kt * 512
                return C_bf[:, b0: b0 + 512]

            def emit_R(E, j):
                if j >= 1:
                    c.wait(E, "dv", f"{tag}_norm{j-1}")
                c.wait(E, "dv", f"{tag}_recip{j}")
                E.matmul(ps_b.ap()[:], ones1b.ap()[:],
                         S_bf[0:1, 3072:3584], start=True, stop=False)
                ins = E.matmul(ps_b.ap()[:], ones1b.ap()[:],
                               S_bf[0:1, 3584:4096], start=False, stop=True)
                c.tick(ins, "pe", 1, f"{tag}_R{j}")

            def emit_den(E, j):
                ins = None
                for kt in range(16):
                    if kt == 0:
                        c.wait(E, "ac", f"{tag}_exp{j}_7")
                        if j >= 1:
                            c.wait(E, "dv", f"{tag}_recip{j-1}")
                    ins = E.matmul(den_slot, onescol.ap()[:, 0:1], p_tile(j, kt),
                                   start=(kt == 0), stop=(kt == 15))
                c.tick(ins, "pe", 1, f"{tag}_den{j}")

            def emit_pv(E, j, kt):
                if kt == 0 and j >= 2:
                    c.wait(E, "dv", f"{tag}_norm{j-2}")
                ins = E.matmul(pv_bank(j), vt_lhsT(kt, j // 16),
                               p_tile(j, kt), start=(kt == 0), stop=(kt == 15))
                if kt == 15:
                    c.tick(ins, "pe", 1, f"{tag}_pv{j}")

            def emit_recip(E, j):
                c.wait(E, "pe", f"{tag}_den{j}")
                E.tensor_scalar(S32[0:1, 0:512], den_slot, SINK, None, ALU.add)
                E.drain()
                E.reciprocal(S32[0:1, 512:1024], S32[0:1, 0:512])
                E.drain()
                E.tensor_copy(S_bf[0:1, 3072:3584], S32[0:1, 512:1024])
                E.drain()
                ins = E.tensor_sub(S_bf[0:1, 3584:4096], S32[0:1, 512:1024],
                                   S_bf[0:1, 3072:3584])
                c.tick(ins, "dv", 1, f"{tag}_recip{j}")

            def emit_norm(E, j):
                c.wait(E, "pe", f"{tag}_R{j}")
                E.tensor_copy(S32[:, 1024:1536], ps_b.ap()[:])
                E.drain()
                hh, qc = divmod(j, 4)
                dst = A_bf[:, hh * 2048 + qc * 512: hh * 2048 + (qc + 1) * 512]
                ins = E.scalar_tensor_tensor(dst, pv_bank(j), 1.0,
                                             S32[:, 1024:1536], ALU.mult, ALU.mult)
                c.tick(ins, "dv", 1, f"{tag}_norm{j}")

            if en == "pe":
                for j in range(NJ):
                    if j >= 2:
                        emit_R(E, j - 2)
                    if j >= 1:
                        emit_den(E, j - 1)
                    for kt in range(16):
                        if kt >= 4:
                            c.wait(E, "ac", f"{tag}_exp{j}_{(kt - 4) // 2}")
                        if j == 0 and kt == 0:
                            c.wait(E, "ac", "a4_qq3done")
                            c.wait(E, "ac", "a4_rms3_15")
                        ins = E.matmul(sT_bank(kt), kT_tile(j, kt), qT_rhs(j),
                                       start=True, stop=True)
                        c.tick(ins, "pe", 1, f"{tag}_sc{j}_{kt}")
                        if j >= 1:
                            emit_pv(E, j - 1, kt)
                emit_R(E, 62)
                emit_den(E, 63)
                for kt in range(16):
                    emit_pv(E, 63, kt)
                emit_R(E, 63)
                c.tick(None, "pe", 0, "a5_done")
            if en == "ac":
                for j in range(NJ):
                    for p in range(8):
                        if p == 0:
                            if j >= 2:
                                c.wait(E, "pe", f"{tag}_pv{j-2}")
                            if j == 0:
                                c.wait(E, "pe", "a4_mm2_15")
                            if j == 1:
                                c.wait(E, "pe", "a4_mm3_15")
                        c.wait(E, "pe", f"{tag}_sc{j}_{2 * p + 1}")
                        ins = E.activation(
                            C_bf[:, (j % 2) * 8192 + p * 1024:
                                 (j % 2) * 8192 + (p + 1) * 1024],
                            ps_big.ap()[:, (p % 2) * 1024:(p % 2) * 1024 + 1024],
                            AF.Exp)
                        c.tick(ins, "ac", 1, f"{tag}_exp{j}_{p}")
            if en == "dv":
                for j in range(NJ):
                    if j >= 2:
                        emit_norm(E, j - 2)
                    if j >= 1:
                        emit_recip(E, j - 1)
                emit_norm(E, 62)
                emit_recip(E, 63)
                emit_norm(E, 63)

        def ph_wo(en, E):
            tag = "a6"
            wosrc = dram["wo"].rearrange("(hc p) o -> p hc o", p=128)
            def _wo_load(E, j):
                if j < 2:
                    c.wait(E, "pe", "a5_sc63_15")
                    dst = B_bf[:, j * 8192:(j + 1) * 8192].rearrange(
                        "p (a b) -> p a b", b=512)
                else:
                    c.wait(E, "pe", f"{tag}_mm{j-2}_15")
                    c.wait(E, "pe", "a5_done")
                    dst = cslot_bf(j % 2).rearrange("p (a b) -> p a b", b=512)
                ins = E.dma_start(dst, wosrc[:, :, j * 512:(j + 1) * 512])
                c.tick(ins, f"w{j % 2}", 16, f"{tag}_w{j}")

            for j in range(4):
                slot = j % 2
                if en == "gp" and j == 0:
                    _wo_load(E, 0)
                    _wo_load(E, 1)
                for tt in range(NT):
                    k = j * NT + tt
                    r = k % 4
                    if en == "pe":
                        if tt == 0:
                            c.wait(E, f"w{j % 2}", f"{tag}_w{j}")
                            if j == 0:
                                c.wait(E, "dv", "a5_norm63")
                        if k >= 4:
                            c.wait(E, "dv", f"{tag}_ev{k-4}")
                        wsrc_bf = (B_bf[:, j * 8192:(j + 1) * 8192] if j < 2
                                   else cslot_bf(slot))
                        ins = None
                        for hc in range(ND):
                            ins = E.matmul(
                                ring_ap(r), AT_lhsT(hc, tt),
                                wsrc_bf[:, hc * 512:(hc + 1) * 512],
                                start=(hc == 0), stop=(hc == ND - 1))
                        c.tick(ins, "pe", 1, f"{tag}_mm{j}_{tt}")
                    if en == "dv":
                        c.wait(E, "pe", f"{tag}_mm{j}_{tt}")
                        if k >= 4:
                            pk = ((k - 4) // 2) * 2
                            c.wait(E, f"st{(pk // 2) % 2}", f"{tag}_stk{pk}")
                        dslot = S32[:, (k % 4) * 512:(k % 4) * 512 + 512]
                        ins = E.tensor_copy(dslot, ring_ap(r))
                        c.tick(ins, "dv", 1, f"{tag}_ev{k}")
                    if en == "gp":
                        if tt == 2 and j >= 1 and j < 3:
                            _wo_load(E, j + 1)
                        if tt % 2 == 1:
                            c.wait(E, "dv", f"{tag}_ev{k}")
                            base = ((k - 1) % 4) * 512
                            src = S32[:, base:base + 1024].rearrange(
                                "p (a b) -> p a b", b=512)
                            ins = E.dma_start(
                                y_view[:, tt - 1: tt + 1, j * 512:(j + 1) * 512],
                                src, accum_op=ALU.add)
                            c.tick(ins, f"st{((k - 1) // 2) % 2}", 16, f"{tag}_stk{k-1}")
            if en == "gp":
                c.tick(None, "st0", 0, "s1_acc_done")
                c.tick(None, "st1", 0, "s1_acc_done2")

        def ph_fin(en, E):
            if c.mode != "emit":
                return
            if en == "gp":
                for s in ("st0", "st1", "w0", "w1", "w2", "w3"):
                    E.wait_ge(c.handles[s], c.cnt[s])
            if en == "sp":
                for s in ("ini", "xe", "xo", "scs", "xp0", "xp1", "xp2", "xp3"):
                    E.wait_ge(c.handles[s], c.cnt[s])

        def run_all(en, E):
            c.reset()
            ph_init(en, E)
            ph_pre0(en, E)
            ph_hT(0, en, E)
            ph_copy(en, E)
            ph_up(0, en, E)
            ph_down(0, en, E)
            ph_hT(1, en, E)
            ph_kv(en, E)
            ph_q(en, E)
            ph_core(en, E)
            ph_wo(en, E)
            ph_hT(2, en, E)
            ph_up(2, en, E)
            ph_down(2, en, E)
            ph_fin(en, E)

        c.mode = "count"
        for en in ENGS:
            c.reset()
            run_all(en, _Null())
        c.check()

        c.mode = "emit"

        @block.sync
        def _(sync):
            run_all("sp", sync)

        @block.gpsimd
        def _(gpsimd):
            run_all("gp", gpsimd)

        @block.tensor
        def _(tensor):
            run_all("pe", tensor)

        @block.scalar
        def _(scalar):
            run_all("ac", scalar)

        @block.vector
        def _(vector):
            run_all("dv", vector)

    return nc


def kernel(**inputs):
    inp = {k: np.asarray(v) for k, v in inputs.items()}
    hs = np.ascontiguousarray(inp["L0_attn_hidden_states"], dtype=np.float32)
    common = {
        "cos": np.ascontiguousarray(inp["L0_attn_cos"][0, :2048, 0, :], dtype=np.float32),
        "sin": np.ascontiguousarray(inp["L0_attn_sin"][0, :2048, 0, :], dtype=np.float32),
        "wq": np.ascontiguousarray(inp["L0_attn_wq"], dtype=np.float32),
        "wk": np.ascontiguousarray(inp["L0_attn_wk"], dtype=np.float32),
        "wv": np.ascontiguousarray(inp["L0_attn_wv"], dtype=np.float32),
        "wo": np.ascontiguousarray(inp["L0_attn_wo"], dtype=np.float32),
        "wupT": np.ascontiguousarray(np.asarray(inp["L0_ffn_W_up"]).T, dtype=np.float32),
        "wdownT": np.ascontiguousarray(np.asarray(inp["L0_ffn_W_down"]).T, dtype=np.float32),
        "wexp": np.ascontiguousarray(inp["L0_ffn_W_expert"], dtype=np.float32),
        "qn": np.ascontiguousarray(inp["L0_attn_q_norm_w"].reshape(1, 128), dtype=np.float32),
        "kn": np.ascontiguousarray(inp["L0_attn_k_norm_w"].reshape(1, 128), dtype=np.float32),
        "identbf": np.eye(128, dtype=ml_dtypes.bfloat16),
        "ident32": np.eye(128, dtype=np.float32),
    }
    nc = build_nc()
    in_maps = [dict(common, x=np.ascontiguousarray(hs[b])) for b in range(8)]
    res = run_bass_kernel_spmd(nc, in_maps, core_ids=list(range(8)))
    out = np.stack([res.results[b]["y"] for b in range(8)], axis=0)
    return out.astype(np.float32)


if __name__ == "__main__":
    nc = build_nc()
    n = sum(1 for _ in nc.all_instructions())
    print("built ok, instructions:", n)


# version 28 (final)
# speedup vs baseline: 1.0643x; 1.0182x over previous
import numpy as np
import ml_dtypes
from contextlib import ExitStack

import concourse.bass as bass
import concourse.mybir as mybir
from concourse.bass_utils import run_bass_kernel_spmd

F32 = mybir.dt.float32
BF16 = mybir.dt.bfloat16
AF = mybir.ActivationFunctionType
ALU = mybir.AluOpType
AX = mybir.AxisListType

NT = 16
ND = 16
DM = 2048
H, KVH, DH = 16, 4, 128
SINK = 640.0
LNC = float(np.log(128.0 ** -0.25))
ENGS = ("sp", "gp", "pe", "ac", "dv")


class _Null:
    def __getattr__(self, k):
        def f(*a, **kw):
            return None
        return f


class Ctx:
    def __init__(self):
        self.mode = "count"
        self.cnt = {}
        self.labels = {}
        self.handles = {}
        self.waited = set()

    def reset(self):
        for k in list(self.cnt):
            self.cnt[k] = 0

    def tick(self, ins, sem, n=1, label=None):
        self.cnt[sem] += n
        if self.mode == "emit" and ins is not None:
            ins.then_inc(self.handles[sem], n)
        if label is not None:
            key = (sem, label)
            if self.mode == "count":
                assert key not in self.labels, f"dup label {key}"
                self.labels[key] = self.cnt[sem]
            else:
                assert self.labels[key] == self.cnt[sem], \
                    f"label drift {key}: {self.labels[key]} vs {self.cnt[sem]}"

    def tickd(self, E, sem, label=None):
        ins = E.drain()
        self.tick(ins, sem, 1, label)

    def wait(self, E, sem, label):
        self.waited.add((sem, label))
        if self.mode == "emit":
            E.wait_ge(self.handles[sem], self.labels[(sem, label)])

    def check(self):
        missing = [k for k in self.waited if k not in self.labels]
        assert not missing, f"missing labels: {sorted(missing)[:8]}"


def build_nc():
    nc = bass.Bass("TRN2", target_bir_lowering=False)

    dram = {}
    dram["x"] = nc.dram_tensor("x", [DM, DM], F32, kind="ExternalInput").ap()
    dram["y"] = nc.dram_tensor("y", [DM, DM], F32, kind="ExternalOutput").ap()
    dram["cos"] = nc.dram_tensor("cos", [DM, DH], F32, kind="ExternalInput").ap()
    dram["sin"] = nc.dram_tensor("sin", [DM, DH], F32, kind="ExternalInput").ap()
    for nm, sh in (("wq", [DM, DM]), ("wk", [DM, 512]), ("wv", [DM, 512]),
                   ("wo", [DM, DM]), ("wupT", [DM, DM]), ("wdownT", [DM, DM]),
                   ("wexp", [DM, DM])):
        dram[nm] = nc.dram_tensor(nm, sh, F32, kind="ExternalInput").ap()
    dram["qn"] = nc.dram_tensor("qn", [1, DH], F32, kind="ExternalInput").ap()
    dram["kn"] = nc.dram_tensor("kn", [1, DH], F32, kind="ExternalInput").ap()
    dram["identbf"] = nc.dram_tensor("identbf", [128, 128], BF16, kind="ExternalInput").ap()
    dram["ident32"] = nc.dram_tensor("ident32", [128, 128], F32, kind="ExternalInput").ap()

    c = Ctx()
    SEMS = ("ini", "xe", "xo", "scs", "w0", "w1", "w2", "w3",
            "st0", "st1", "pe", "ac", "dv", "xp0", "xp1", "xp2", "xp3")
    for s in SEMS:
        c.cnt[s] = 0

    with ExitStack() as stack:
        e = stack.enter_context
        A = e(nc.sbuf_tensor("arenaA", [128, 16384], F32))
        B = e(nc.sbuf_tensor("arenaB", [128, 16384], F32))
        C = e(nc.sbuf_tensor("arenaC", [128, 16384], F32))
        S = e(nc.sbuf_tensor("arenaS", [128, 2080], F32))
        negm_all = e(nc.sbuf_tensor("negm_all", [128, 256], F32))
        qn_bc = e(nc.sbuf_tensor("qn_bc", [128, 128], F32))
        kn_bc = e(nc.sbuf_tensor("kn_bc", [128, 128], F32))
        identbf = e(nc.sbuf_tensor("identbf_sb", [128, 128], BF16))
        ident32 = e(nc.sbuf_tensor("ident32_sb", [128, 128], F32))
        ones = e(nc.sbuf_tensor("ones_sb", [1, 128], F32))
        onescol = e(nc.sbuf_tensor("onescol_sb", [128, 2], BF16))
        ones1b = e(nc.sbuf_tensor("ones1b_sb", [1, 128], BF16))
        epsc = e(nc.sbuf_tensor("epsc", [128, 4], F32))
        stt = e(nc.sbuf_tensor("stats", [128, 64], F32))
        ssbk_all = e(nc.sbuf_tensor("ssbk_all", [128, 64], F32))
        mkh = e(nc.sbuf_tensor("mkh", [128, 4], F32))
        junkA = e(nc.sbuf_tensor("junkA", [128, 512], F32))
        e_all = e(nc.sbuf_tensor("e_all", [128, 256], F32))
        ps_big = e(nc.psum_tensor("ps_big", [128, 2048], F32))
        ps_tt = e(nc.psum_tensor("ps_tt", [128, 2048], BF16))
        ps_a = e(nc.psum_tensor("ps_a", [128, 512], F32))
        ps_b = e(nc.psum_tensor("ps_b", [128, 512], F32))

        pstt32 = ps_tt.ap().bitcast(F32)

        def ring_ap(r):
            if r == 0:
                return ps_a.ap()[:]
            if r == 1:
                return ps_b.ap()[:]
            if r == 2:
                return ps_big.ap()[:, 0:512]
            return ps_big.ap()[:, 512:1024]

        def pst_ap(parity):
            return ps_tt.ap()[:, parity * 1024: parity * 1024 + 1024]
        for s in SEMS:
            c.handles[s] = e(nc.semaphore(f"sem_{s}"))
        block = e(nc.Block())

        A32, B32, C32, S32 = A.ap(), B.ap(), C.ap(), S.ap()
        A_bf = A32.bitcast(BF16)
        B_bf = B32.bitcast(BF16)
        C_bf = C32.bitcast(BF16)
        S_bf = S32.bitcast(BF16)

        def col(t, j, w=1):
            return t.ap()[:, j:j + w]

        qn_rowV = junkA.ap()[0:1, 256:384]
        kn_rowV = junkA.ap()[0:1, 384:512]
        tr64V = junkA.ap()[0:64, 0:128]
        t2cV = junkA.ap()[0:64, 192:193]
        t3rV = junkA.ap()[0:1, 128:192]
        mkn2V = junkA.ap()[0:1, 200:204]

        def hT_lhsT(dc, tt):
            return A_bf[:, dc * 2048 + tt * 128: dc * 2048 + (tt + 1) * 128]

        def hT_rhs(dc, tc):
            return A_bf[:, dc * 2048 + tc * 512: dc * 2048 + (tc + 1) * 512]

        def AT_lhsT(hc, tt):
            return A_bf[:, hc * 2048 + tt * 128: hc * 2048 + (tt + 1) * 128]

        def uT_lhsT(uc, tt):
            return B_bf[:, uc * 2048 + tt * 128: uc * 2048 + (tt + 1) * 128]

        def qT_lhsT(h, qg):
            return B_bf[:, h * 2048 + qg * 128: h * 2048 + (qg + 1) * 128]

        def cslot_bf(k):
            return C_bf[:, k * 8192:(k + 1) * 8192]

        def kT_rhs(kvh, kc):
            b0 = 16384 + kvh * 2048 + kc * 512
            return C_bf[:, b0: b0 + 512]

        def vt_lhsT(kc, kvh):
            b0 = 24576 + kc * 512 + kvh * 128
            return C_bf[:, b0: b0 + 128]

        def p_slot(par):
            return C_bf[:, par * 2048:(par + 1) * 2048]

        def pT_rhs(kc):
            return C_bf[:, 4096 + kc * 512: 4096 + (kc + 1) * 512]

        def xsrc(sub):
            return dram["x"] if sub == 0 else dram["y"]

        y_view = dram["y"].rearrange("(t p) o -> p t o", p=128)

        UPSCHED = [(0, 0), (1, 0), (2, 0), (3, 0), (4, 0), (5, 0), (6, 0),
                   (7, 0), (0, 1), (1, 1), (2, 1)]

        def up_ring(ot, tc):
            if ot % 2 == 0:
                return ps_big.ap()[:, tc * 512:(tc + 1) * 512]
            return (ps_a.ap()[:], ps_b.ap()[:],
                    pstt32[:, 0:512], pstt32[:, 512:1024])[tc]

        def emit_up_chunk(E, sub, ot, tc):
            tag = f"s{sub}u"
            q = ot // 4
            c.wait(E, f"w{q % 2}", f"{tag}_w{q}")
            c.wait(E, "dv", f"s{sub}_hTe{16 * tc + 15}")
            if ot == 1 and tc == 0:
                c.wait(E, "ac", "gains_bc")
            if ot >= 2:
                c.wait(E, "ac", f"{tag}_relu{ot-2}_{tc}")
            ins = None
            for dc in range(ND):
                lhsT = cslot_bf(q % 2)[:, dc * 512 + (ot % 4) * 128:
                                       dc * 512 + (ot % 4 + 1) * 128]
                ins = E.matmul(up_ring(ot, tc), lhsT, hT_rhs(dc, tc),
                               start=(dc == 0), stop=(dc == ND - 1))
            c.tick(ins, "pe", 1, f"{tag}_cmm{ot}_{tc}")

        def emit_up_relu(E, sub, ot, tc):
            tag = f"s{sub}u"
            c.wait(E, "pe", f"{tag}_cmm{ot}_{tc}")
            ins = E.activation(
                B_bf[:, ot * 2048 + tc * 512: ot * 2048 + (tc + 1) * 512],
                up_ring(ot, tc), AF.Relu)
            c.tick(ins, "ac", 1, f"{tag}_relu{ot}_{tc}")

        def ph_init(en, E):
            if en == "sp":
                dsts = (("identbf", identbf.ap()[:]), ("ident32", ident32.ap()[:]),
                        ("qn", qn_rowV), ("kn", kn_rowV))
                for i, (nm, dst) in enumerate(dsts):
                    ins = E.dma_start(dst, dram[nm])
                    c.tick(ins, "ini", 16, "ld_all" if i == 3 else None)
            if en == "dv":
                E.memset(ones.ap()[:], 1.0)
                E.memset(onescol.ap()[:], 1.0)
                E.memset(ones1b.ap()[:], 1.0)
                E.memset(col(epsc, 0), 1e-5)
                E.memset(col(epsc, 1), 1e-6)
                E.memset(col(epsc, 2), LNC)
                c.tickd(E, "dv", "init_consts")
            if en == "pe":
                c.wait(E, "ini", "ld_all")
                c.wait(E, "dv", "init_consts")
                E.matmul(ps_b.ap()[:, 0:128], ones.ap()[:], qn_rowV,
                         start=True, stop=True)
                ins = E.matmul(ps_b.ap()[:, 128:256], ones.ap()[:], kn_rowV,
                               start=True, stop=True)
                c.tick(ins, "pe", 1, "gains_mm")
            if en == "ac":
                c.wait(E, "pe", "gains_mm")
                E.activation(qn_bc.ap()[:], ps_b.ap()[:, 0:128], AF.Identity)
                E.activation(kn_bc.ap()[:], ps_b.ap()[:, 128:256], AF.Identity)
                c.tickd(E, "ac", "gains_bc")

        def ph_pre0(en, E):
            if en == "sp":
                for tt in range(4):
                    ins = E.dma_start(C32[:, 8192 + tt * 2048: 8192 + (tt + 1) * 2048],
                                      dram["x"][tt * 128:(tt + 1) * 128, :])
                    c.tick(ins, f"xp{tt}", 16, f"s0_x{tt}")

        def ph_copy(en, E):
            if en == "sp":
                for i in range(4):
                    ins = E.dma_start(dram["y"][i * 512:(i + 1) * 512, :],
                                      dram["x"][i * 512:(i + 1) * 512, :])
                    c.tick(ins, "ini", 16, "xy_copy3" if i == 3 else None)

        def ph_hT(sub, en, E):
            tag = f"s{sub}"
            src = xsrc(sub)
            npre = 4 if sub in (0, 1) else 0

            def xbuf(tt):
                if sub != 1:
                    s4 = tt % 4
                    return C32[:, 8192 + s4 * 2048: 8192 + (s4 + 1) * 2048]
                if tt < npre:
                    return C32[:, 8192 + tt * 2048: 8192 + (tt + 1) * 2048]
                b0 = (0, 2048, 6144, 8192)[tt % 4]
                return B32[:, b0: b0 + 2048]

            def hbuf(tt):
                b0 = (8192 if sub == 1 else 24576) + (tt % 2) * 2048
                return B_bf[:, b0: b0 + 2048]

            for tt in range(NT):
                sb = (tt % 4) * 8
                if en == "sp":
                    if sub == 1 and tt >= npre:
                        c.wait(E, "st0", "s0_acc_done")
                        c.wait(E, "st1", "s0_acc_done2")
                    if sub == 2:
                        for j4 in range(4):
                            kk = j4 * NT + tt
                            pr = kk - kk % 2
                            c.wait(E, f"st{(pr // 2) % 2}", f"a6_stk{pr}")
                    if tt >= npre:
                        if tt >= (8 if sub == 1 else 4):
                            c.wait(E, "ac", f"{tag}_happ{tt-4}")
                        ins = E.dma_start(xbuf(tt), src[tt * 128:(tt + 1) * 128, :])
                        c.tick(ins, "xe" if tt % 2 == 0 else "xo", 16, f"{tag}_x{tt}")
                if en == "dv":
                    xs = (f"xp{tt}" if tt < npre else ("xe" if tt % 2 == 0 else "xo"))
                    c.wait(E, xs, f"{tag}_x{tt}")
                    if tt >= 4:
                        c.wait(E, "ac", f"{tag}_happ{tt-4}")
                    for i4 in range(4):
                        E.bn_stats(junkA.ap()[:, i4 * 6:(i4 + 1) * 6],
                                   xbuf(tt)[:, i4 * 512:(i4 + 1) * 512])
                    E.drain()
                    E.bn_aggr(col(stt, sb + 0, 2), junkA.ap()[:, 0:24])
                    c.tickd(E, "dv", f"{tag}_ssum{tt}")
                    if tt >= 1:
                        for g in range(4):
                            gid = (tt - 1) * 4 + g
                            gp2 = gid % 2
                            c.wait(E, "pe", f"{tag}_hT{tt-1}g{g}")
                            dst = A_bf.rearrange("p (a b) -> p a b", b=2048)[
                                :, g * 4:(g + 1) * 4, (tt - 1) * 128:tt * 128]
                            src_ps = pst_ap(gp2)[:, 0:512].rearrange(
                                "p (a b) -> p a b", b=128)
                            ins = E.tensor_copy(dst, src_ps)
                            c.tick(ins, "dv", 1, f"{tag}_hTe{gid}")
                if en == "ac":
                    c.wait(E, "dv", f"{tag}_ssum{tt}")
                    E.activation(col(stt, sb + 7), col(stt, sb + 1), AF.Ln,
                                 bias=col(epsc, 0))
                    E.drain()
                    E.activation(col(stt, sb + 6), col(stt, sb + 7), AF.Exp,
                                 scale=-0.5)
                    E.drain()
                    E.activation(col(stt, sb + 3), col(stt, sb + 0), AF.Identity,
                                 scale=col(stt, sb + 6))
                    E.drain()
                    E.activation(col(stt, sb + 5), col(stt, sb + 3), AF.Identity,
                                 scale=-1.0)
                    E.drain()
                    if tt >= 2:
                        c.wait(E, "pe", f"{tag}_hT{tt-2}g3")
                    for hc4 in range(4):
                        ins = E.activation(hbuf(tt)[:, hc4 * 512:(hc4 + 1) * 512],
                                           xbuf(tt)[:, hc4 * 512:(hc4 + 1) * 512],
                                           AF.Identity,
                                           scale=col(stt, sb + 6), bias=col(stt, sb + 5))
                        c.tick(ins, "ac", 1, f"{tag}_happ{tt}c{hc4}")
                    c.tick(None, "ac", 0, f"{tag}_happ{tt}")
                    if sub != 1 and tt >= 7:
                        emit_up_relu(E, sub, *UPSCHED[tt - 7])
                if en == "pe":
                    for g in range(4):
                        gid = tt * 4 + g
                        gp2 = gid % 2
                        c.wait(E, "ac", f"{tag}_happ{tt}c{g}")
                        if gid >= 2:
                            c.wait(E, "dv", f"{tag}_hTe{gid-2}")
                        ins = None
                        for j in range(4):
                            dc = g * 4 + j
                            ins = E.transpose(
                                pst_ap(gp2)[:, j * 128:(j + 1) * 128],
                                hbuf(tt)[:, dc * 128:(dc + 1) * 128],
                                identbf.ap()[:])
                        c.tick(ins, "pe", 1, f"{tag}_hT{tt}g{g}")
                    if sub != 1 and tt >= 5:
                        emit_up_chunk(E, sub, *UPSCHED[tt - 5])
            if en == "dv":
                for g in range(4):
                    gid = (NT - 1) * 4 + g
                    gp2 = gid % 2
                    c.wait(E, "pe", f"{tag}_hT{NT-1}g{g}")
                    dst = A_bf.rearrange("p (a b) -> p a b", b=2048)[
                        :, g * 4:(g + 1) * 4, (NT - 1) * 128:NT * 128]
                    src_ps = pst_ap(gp2)[:, 0:512].rearrange(
                        "p (a b) -> p a b", b=128)
                    ins = E.tensor_copy(dst, src_ps)
                    c.tick(ins, "dv", 1, f"{tag}_hTe{gid}")
                c.tick(None, "dv", 0, f"{tag}_hT_done")

        def ph_up(sub, en, E):
            tag = f"s{sub}u"
            wsrc = dram["wupT"].rearrange("(dc p) o -> p dc o", p=128)
            rest = [(ot, tc) for ot in range(16) for tc in range(4)
                    if (ot, tc) not in UPSCHED]
            if en == "gp":
                for q in range(4):
                    slot = q % 2
                    if q >= 2:
                        c.wait(E, "pe", f"{tag}_q{q-2}done")
                    if sub == 2 and q < 2:
                        c.wait(E, "pe", "a5_done")
                    dst = cslot_bf(slot).rearrange("p (a b) -> p a b", b=512)
                    for half in range(2):
                        ins = E.dma_start(
                            dst[:, half * 8:(half + 1) * 8, :],
                            wsrc[:, half * 8:(half + 1) * 8, q * 512:(q + 1) * 512])
                        c.tick(ins, f"w{q % 2}", 16,
                               f"{tag}_w{q}" if half == 1 else None)
            if en == "pe":
                for (ot, tc) in rest:
                    emit_up_chunk(E, sub, ot, tc)
                    if ot % 4 == 3 and tc == 3:
                        c.tick(None, "pe", 0, f"{tag}_q{ot // 4}done")
            if en == "ac":
                for (ot, tc) in UPSCHED[9:]:
                    emit_up_relu(E, sub, ot, tc)
                for (ot, tc) in rest:
                    emit_up_relu(E, sub, ot, tc)
                c.tick(None, "ac", 0, f"{tag}_done")

        def ph_down(sub, en, E):
            tag = f"s{sub}d"
            wd = dram["wdownT"].rearrange("(uc p) o -> p uc o", p=128)
            we = dram["wexp"].rearrange("(dc p) o -> p dc o", p=128)
            scale = 0.5
            def _dn_loads(E, j):
                sl_ = ((j + 1) % 2) * 2
                if j >= 2:
                    c.wait(E, "pe", f"{tag}_j{j-2}done")
                if sl_ == 0:
                    c.wait(E, "pe", f"s{sub}u_q3done")
                if sl_ == 2 and j == 0:
                    c.wait(E, "ac", f"s{sub}_happ15")
                if sub == 2 and sl_ == 2 and j == 0:
                    c.wait(E, "pe", "a5_done")
                for io, (nm, srcv) in enumerate((("d", wd), ("e", we))):
                    dst = cslot_bf(sl_ + io).rearrange("p (a b) -> p a b", b=512)
                    ins = E.dma_start(dst, srcv[:, :, j * 512:(j + 1) * 512])
                    c.tick(ins, f"w{sl_ + io}", 16, f"{tag}_w{nm}{j}")

            if en == "sp" and sub == 0:
                c.wait(E, "pe", f"{tag}_j2done")
                for tt in range(4):
                    for j4 in range(4):
                        kk = j4 * NT + tt
                        pr = kk - kk % 2
                        c.wait(E, f"st{(pr // 2) % 2}", f"{tag}_stk{pr}")
                    ins = E.dma_start(C32[:, 8192 + tt * 2048: 8192 + (tt + 1) * 2048],
                                      dram["y"][tt * 128:(tt + 1) * 128, :])
                    c.tick(ins, f"xp{tt}", 16, f"s1_x{tt}")
            for j in range(4):
                sl = ((j + 1) % 2) * 2
                if en == "gp" and j == 0:
                    _dn_loads(E, 0)
                    _dn_loads(E, 1)
                for tt in range(NT):
                    k = j * NT + tt
                    r = k % 4
                    if en == "pe":
                        if tt == 0:
                            c.wait(E, f"w{sl}", f"{tag}_wd{j}")
                            c.wait(E, f"w{sl+1}", f"{tag}_we{j}")
                            if j == 0:
                                c.wait(E, "ac", f"s{sub}u_done")
                                c.wait(E, "ac", "gains_bc")
                        if k >= 4:
                            c.wait(E, "dv", f"{tag}_ev{k-4}")
                        ins = None
                        for uc in range(ND):
                            ins = E.matmul(ring_ap(r), uT_lhsT(uc, tt),
                                           cslot_bf(sl)[:, uc * 512:(uc + 1) * 512],
                                           start=(uc == 0), stop=False)
                        for dc in range(ND):
                            ins = E.matmul(ring_ap(r), hT_lhsT(dc, tt),
                                           cslot_bf(sl + 1)[:, dc * 512:(dc + 1) * 512],
                                           start=False, stop=(dc == ND - 1))
                        c.tick(ins, "pe", 1, f"{tag}_mm{j}_{tt}")
                        if tt == NT - 1:
                            c.tick(None, "pe", 0, f"{tag}_j{j}done")
                    if en == "dv":
                        c.wait(E, "pe", f"{tag}_mm{j}_{tt}")
                        if k >= 4:
                            pk = ((k - 4) // 2) * 2
                            c.wait(E, f"st{(pk // 2) % 2}", f"{tag}_stk{pk}")
                        dslot = S32[:, (k % 4) * 512:(k % 4) * 512 + 512]
                        ins = E.tensor_scalar(dslot, ring_ap(r), scale, None,
                                              ALU.mult)
                        c.tick(ins, "dv", 1, f"{tag}_ev{k}")
                    if en == "gp":
                        if tt == 2 and j >= 1 and j < 3:
                            _dn_loads(E, j + 1)
                        if tt % 2 == 1:
                            if sub == 0 and j == 0 and tt == 1:
                                c.wait(E, "ini", "xy_copy3")
                            c.wait(E, "dv", f"{tag}_ev{k}")
                            base = ((k - 1) % 4) * 512
                            src = S32[:, base:base + 1024].rearrange(
                                "p (a b) -> p a b", b=512)
                            ins = E.dma_start(
                                y_view[:, tt - 1: tt + 1, j * 512:(j + 1) * 512],
                                src, accum_op=ALU.add)
                            c.tick(ins, f"st{((k - 1) // 2) % 2}", 16, f"{tag}_stk{k-1}")
            if en == "gp":
                c.tick(None, "st0", 0, f"s{sub}_acc_done")
                c.tick(None, "st1", 0, f"s{sub}_acc_done2")

        def ph_kv(en, E):
            tag = "a2"
            wksrc = dram["wk"].rearrange("(dc p) o -> p dc o", p=128)
            wvsrc = dram["wv"].rearrange("(dc p) o -> p dc o", p=128)
            if en == "gp":
                c.wait(E, "pe", "s0d_j3done")
                for nm, srcv, kk in (("wk", wksrc, 0), ("wv", wvsrc, 1)):
                    dst = cslot_bf(kk).rearrange("p (a b) -> p a b", b=512)
                    ins = E.dma_start(dst, srcv)
                    c.tick(ins, f"w{kk}", 16, f"{tag}_{nm}")

            def krm_ap(tt):
                return S32[:, (tt % 2) * 512:(tt % 2) * 512 + 512]

            def kf_ap(tt):
                b0 = 3072 + (tt % 2) * 512
                return S_bf[:, b0: b0 + 512]

            def emit_kT(E, tt):
                c.wait(E, "dv", f"{tag}_rope{tt}")
                if tt >= 2:
                    c.wait(E, "ac", f"{tag}_kTe{tt-2}")
                ins = None
                for kv in range(KVH):
                    ins = E.transpose(
                        pst_ap(tt % 2)[:, kv * 128:(kv + 1) * 128],
                        kf_ap(tt)[:, kv * 128:(kv + 1) * 128],
                        identbf.ap()[:])
                c.tick(ins, "pe", 1, f"{tag}_kT{tt}")

            def emit_kTe(E, tt):
                c.wait(E, "pe", f"{tag}_kT{tt}")
                dst = C_bf[:, 16384:24576].rearrange("p (a b) -> p a b", b=2048)[
                    :, :, tt * 128:(tt + 1) * 128]
                ins = E.activation(dst,
                                   pst_ap(tt % 2)[:, 0:512].rearrange(
                                       "p (a b) -> p a b", b=128), AF.Identity)
                c.tick(ins, "ac", 1, f"{tag}_kTe{tt}")

            for tt in range(NT):
                g2 = tt // 2
                psk_ap = ps_a.ap()[:] if tt % 2 == 0 else ps_big.ap()[:, 0:512]
                psv_ap = ps_b.ap()[:] if tt % 2 == 0 else ps_big.ap()[:, 512:1024]
                sb = 16 + (tt % 2) * 8
                if en == "sp" and tt % 2 == 0:
                    if tt >= 2:
                        c.wait(E, "dv", f"{tag}_rope{tt-1}")
                    for nm in ("cos", "sin"):
                        base = 1024 if nm == "cos" else 1280
                        ins = E.dma_start(
                            S32[:, base:base + 256].rearrange("p (a b) -> p a b", b=128),
                            dram[nm].rearrange("(a p) d -> p a d", p=128)[
                                :, g2 * 2:(g2 + 1) * 2, :])
                        c.tick(ins, "scs", 16,
                               f"{tag}_cs{g2}" if nm == "sin" else None)
                if en == "pe":
                    if tt == 0:
                        c.wait(E, "w0", f"{tag}_wk")
                        c.wait(E, "w1", f"{tag}_wv")
                        c.wait(E, "ac", "gains_bc")
                    c.wait(E, "dv", f"s1_hTe{4 * tt + 3}")
                    if tt >= 2:
                        c.wait(E, "ac", f"{tag}_rms{tt-2}")
                        c.wait(E, "ac", f"{tag}_vev{tt-2}")
                    ins = None
                    for dc in range(ND):
                        E.matmul(psk_ap, hT_lhsT(dc, tt),
                                 cslot_bf(0)[:, dc * 512:(dc + 1) * 512],
                                 start=(dc == 0), stop=(dc == ND - 1))
                        ins = E.matmul(psv_ap, hT_lhsT(dc, tt),
                                       cslot_bf(1)[:, dc * 512:(dc + 1) * 512],
                                       start=(dc == 0), stop=(dc == ND - 1))
                    c.tick(ins, "pe", 1, f"{tag}_mm{tt}")
                    if tt >= 1:
                        emit_kT(E, tt - 1)
                if en == "ac":
                    c.wait(E, "pe", f"{tag}_mm{tt}")
                    ins = E.activation(C_bf[:, 24576 + tt * 512: 24576 + (tt + 1) * 512],
                                        psv_ap, AF.Identity)
                    c.tick(ins, "ac", 1, f"{tag}_vev{tt}")
                    if tt >= 2:
                        c.wait(E, "dv", f"{tag}_rope{tt-2}")
                    for kv in range(KVH):
                        E.activation(krm_ap(tt)[:, kv * 128:(kv + 1) * 128],
                                     psk_ap[:, kv * 128:(kv + 1) * 128],
                                     AF.Square, accum_out=col(stt, sb + kv))
                    E.drain()
                    E.activation(col(stt, sb + 4, 4), col(stt, sb, 4), AF.Ln,
                                 scale=1.0 / 128.0, bias=col(epsc, 1))
                    E.drain()
                    E.activation(col(stt, sb, 4), col(stt, sb + 4, 4), AF.Exp,
                                 scale=-0.5, bias=col(epsc, 2))
                    E.drain()
                    ins = None
                    for kv in range(KVH):
                        ins = E.activation(krm_ap(tt)[:, kv * 128:(kv + 1) * 128],
                                           psk_ap[:, kv * 128:(kv + 1) * 128],
                                           AF.Identity, scale=col(stt, sb + kv))
                    c.tick(ins, "ac", 1, f"{tag}_rms{tt}")
                    if tt >= 1:
                        emit_kTe(E, tt - 1)
                if en == "dv":
                    c.wait(E, "ac", f"{tag}_rms{tt}")
                    c.wait(E, "ac", "gains_bc")
                    krm = krm_ap(tt).rearrange("p (h d) -> p h d", d=128)
                    E.tensor_mul(krm, krm,
                                 kn_bc.ap()[:, None, :].broadcast_to([128, KVH, 128]))
                    E.drain()
                    c.wait(E, "scs", f"{tag}_cs{g2}")
                    if tt >= 2:
                        c.wait(E, "pe", f"{tag}_kT{tt-2}")
                    cosr = S32[:, 1024 + (tt % 2) * 128: 1024 + (tt % 2 + 1) * 128][
                        :, None, :].broadcast_to([128, KVH, 128])
                    sinr = S32[:, 1280 + (tt % 2) * 128: 1280 + (tt % 2 + 1) * 128][
                        :, None, :].broadcast_to([128, KVH, 128])
                    kf = kf_ap(tt).rearrange("p (h d) -> p h d", d=128)
                    ju = junkA.ap()[:, 0:512].rearrange("p (h d) -> p h d", d=128)
                    E.tensor_mul(kf[:, :, 0:64], krm[:, :, 64:128], sinr[:, :, 0:64])
                    E.tensor_mul(kf[:, :, 64:128], krm[:, :, 0:64], sinr[:, :, 64:128])
                    E.tensor_mul(ju, krm, cosr)
                    E.drain()
                    E.tensor_sub(kf[:, :, 0:64], ju[:, :, 0:64], kf[:, :, 0:64])
                    E.tensor_add(kf[:, :, 64:128], ju[:, :, 64:128], kf[:, :, 64:128])
                    c.tickd(E, "dv", f"{tag}_rope{tt}")
            if en == "pe":
                emit_kT(E, NT - 1)
            if en == "ac":
                emit_kTe(E, NT - 1)

        def ph_q(en, E):
            tag = "a4"
            wqsrc = dram["wq"].rearrange("(dc p) o -> p dc o", p=128)

            def qrm_ap(k):
                return S32[:, (k % 2) * 512:(k % 2) * 512 + 512]

            def qf_ap(k):
                b0 = 3072 + (k % 2) * 512
                return S_bf[:, b0: b0 + 512]

            def emit_qT(E, k):
                pq, ptt = divmod(k, NT)
                c.wait(E, "dv", f"{tag}_rope{pq}_{ptt}")
                if k >= 2:
                    p2q, p2t = divmod(k - 2, NT)
                    c.wait(E, "ac", f"{tag}_qTe{p2q}_{p2t}")
                ins = None
                for hj in range(4):
                    ins = E.transpose(
                        pst_ap(k % 2)[:, hj * 128:(hj + 1) * 128],
                        qf_ap(k)[:, hj * 128:(hj + 1) * 128],
                        identbf.ap()[:])
                c.tick(ins, "pe", 1, f"{tag}_qT{pq}_{ptt}")

            def emit_qTe(E, k):
                pq, ptt = divmod(k, NT)
                c.wait(E, "pe", f"{tag}_qT{pq}_{ptt}")
                dst = B_bf.rearrange("p (a b) -> p a b", b=2048)[
                    :, pq * 4:(pq + 1) * 4, ptt * 128:(ptt + 1) * 128]
                ins = E.activation(dst,
                                   pst_ap(k % 2)[:, 0:512].rearrange(
                                       "p (a b) -> p a b", b=128), AF.Identity)
                c.tick(ins, "ac", 1, f"{tag}_qTe{pq}_{ptt}")

            for qq in range(4):
                slot = qq % 2
                if en == "gp":
                    if qq < 2:
                        c.wait(E, "pe", "a2_mm15")
                    else:
                        c.wait(E, "pe", f"{tag}_mm{qq-2}_15")
                    dst = cslot_bf(slot).rearrange("p (a b) -> p a b", b=512)
                    ins = E.dma_start(dst, wqsrc[:, :, qq * 512:(qq + 1) * 512])
                    c.tick(ins, f"w{qq % 2}", 16, f"{tag}_w{qq}")
                for tt in range(NT):
                    g2 = tt // 2
                    k = qq * NT + tt
                    r = k % 4
                    sb = 32 + (tt % 2) * 8
                    if en == "sp" and tt % 2 == 0:
                        if k > 0:
                            pq, ptt = divmod(k - 1, NT)
                            c.wait(E, "dv", f"{tag}_rope{pq}_{ptt}")
                        else:
                            c.wait(E, "dv", "a2_rope15")
                        for nm in ("cos", "sin"):
                            base = 1024 if nm == "cos" else 1280
                            ins = E.dma_start(
                                S32[:, base:base + 256].rearrange("p (a b) -> p a b", b=128),
                                dram[nm].rearrange("(a p) d -> p a d", p=128)[
                                    :, g2 * 2:(g2 + 1) * 2, :])
                            c.tick(ins, "scs", 16,
                                   f"{tag}_cs{qq}_{g2}" if nm == "sin" else None)
                    if en == "pe":
                        if tt == 0:
                            c.wait(E, "w" + str(qq % 2), f"{tag}_w{qq}")
                            if qq == 0:
                                c.wait(E, "ac", "a2_rms15")
                        if k >= 4:
                            pq, ptt = divmod(k - 4, NT)
                            c.wait(E, "ac", f"{tag}_rms{pq}_{ptt}")
                        ins = None
                        for dc in range(ND):
                            ins = E.matmul(ring_ap(r), hT_lhsT(dc, tt),
                                           cslot_bf(slot)[:, dc * 512:(dc + 1) * 512],
                                           start=(dc == 0), stop=(dc == ND - 1))
                        c.tick(ins, "pe", 1, f"{tag}_mm{qq}_{tt}")
                        if k >= 2:
                            emit_qT(E, k - 2)
                    if en == "ac":
                        c.wait(E, "pe", f"{tag}_mm{qq}_{tt}")
                        if k >= 2:
                            pq, ptt = divmod(k - 2, NT)
                            c.wait(E, "dv", f"{tag}_rope{pq}_{ptt}")
                        else:
                            c.wait(E, "dv", "a2_rope15")
                        for hj in range(4):
                            E.activation(qrm_ap(k)[:, hj * 128:(hj + 1) * 128],
                                         ring_ap(r)[:, hj * 128:(hj + 1) * 128],
                                         AF.Square, accum_out=col(stt, sb + hj))
                        E.drain()
                        E.activation(col(stt, sb + 4, 4), col(stt, sb, 4), AF.Ln,
                                     scale=1.0 / 128.0, bias=col(epsc, 1))
                        E.drain()
                        E.activation(col(stt, sb, 4), col(stt, sb + 4, 4), AF.Exp,
                                     scale=-0.5, bias=col(epsc, 2))
                        E.drain()
                        ins = None
                        for hj in range(4):
                            ins = E.activation(qrm_ap(k)[:, hj * 128:(hj + 1) * 128],
                                               ring_ap(r)[:, hj * 128:(hj + 1) * 128],
                                               AF.Identity, scale=col(stt, sb + hj))
                        c.tick(ins, "ac", 1, f"{tag}_rms{qq}_{tt}")
                        if k >= 2:
                            emit_qTe(E, k - 2)
                    if en == "dv":
                        c.wait(E, "ac", f"{tag}_rms{qq}_{tt}")
                        qrm = qrm_ap(k).rearrange("p (h d) -> p h d", d=128)
                        E.tensor_mul(qrm, qrm,
                                     qn_bc.ap()[:, None, :].broadcast_to([128, 4, 128]))
                        E.drain()
                        c.wait(E, "scs", f"{tag}_cs{qq}_{g2}")
                        if k >= 2:
                            pq, ptt = divmod(k - 2, NT)
                            c.wait(E, "pe", f"{tag}_qT{pq}_{ptt}")
                        else:
                            c.wait(E, "pe", "a2_kT15")
                        cosr = S32[:, 1024 + (tt % 2) * 128: 1024 + (tt % 2 + 1) * 128][
                            :, None, :].broadcast_to([128, 4, 128])
                        sinr = S32[:, 1280 + (tt % 2) * 128: 1280 + (tt % 2 + 1) * 128][
                            :, None, :].broadcast_to([128, 4, 128])
                        qf = qf_ap(k).rearrange("p (h d) -> p h d", d=128)
                        ju = junkA.ap()[:, 0:512].rearrange("p (h d) -> p h d", d=128)
                        E.tensor_mul(qf[:, :, 0:64], qrm[:, :, 64:128], sinr[:, :, 0:64])
                        E.tensor_mul(qf[:, :, 64:128], qrm[:, :, 0:64], sinr[:, :, 64:128])
                        E.tensor_mul(ju, qrm, cosr)
                        E.drain()
                        E.tensor_sub(qf[:, :, 0:64], ju[:, :, 0:64], qf[:, :, 0:64])
                        E.tensor_add(qf[:, :, 64:128], ju[:, :, 64:128], qf[:, :, 64:128])
                        c.tickd(E, "dv", f"{tag}_rope{qq}_{tt}")
            if en == "pe":
                emit_qT(E, 62)
                emit_qT(E, 63)
            if en == "ac":
                emit_qTe(E, 62)
                emit_qTe(E, 63)
                c.tick(None, "ac", 0, "a4_qq3done")

        def ph_core(en, E):
            tag = "a5"
            NJ = 64
            den_slot = pstt32[0:1, 512:1024]

            def sT_bank(kt):
                return ps_big.ap()[:, (kt % 4) * 512:(kt % 4) * 512 + 512]

            def pv_bank(j):
                return (ps_a.ap()[:] if j % 2 == 0 else pstt32[:, 0:512])

            def kT_tile(j, kt):
                b0 = 16384 + (j // 16) * 2048 + kt * 128
                return C_bf[:, b0: b0 + 128]

            def qT_rhs(j):
                hh, qc = divmod(j, 4)
                return B_bf[:, hh * 2048 + qc * 512: hh * 2048 + (qc + 1) * 512]

            def p_tile(j, kt):
                b0 = (j % 2) * 8192 + kt * 512
                return C_bf[:, b0: b0 + 512]

            def emit_R(E, j):
                if j >= 1:
                    c.wait(E, "dv", f"{tag}_norm{j-1}")
                c.wait(E, "dv", f"{tag}_recip{j}")
                E.matmul(ps_b.ap()[:], ones1b.ap()[:],
                         S_bf[0:1, 3072:3584], start=True, stop=False)
                ins = E.matmul(ps_b.ap()[:], ones1b.ap()[:],
                               S_bf[0:1, 3584:4096], start=False, stop=True)
                c.tick(ins, "pe", 1, f"{tag}_R{j}")

            def emit_den(E, j):
                ins = None
                for kt in range(16):
                    if kt == 0:
                        c.wait(E, "ac", f"{tag}_exp{j}_7")
                        if j >= 1:
                            c.wait(E, "dv", f"{tag}_recip{j-1}")
                    ins = E.matmul(den_slot, onescol.ap()[:, 0:1], p_tile(j, kt),
                                   start=(kt == 0), stop=(kt == 15))
                c.tick(ins, "pe", 1, f"{tag}_den{j}")

            def emit_pv(E, j, kt):
                if kt == 0 and j >= 2:
                    c.wait(E, "dv", f"{tag}_norm{j-2}")
                ins = E.matmul(pv_bank(j), vt_lhsT(kt, j // 16),
                               p_tile(j, kt), start=(kt == 0), stop=(kt == 15))
                if kt == 15:
                    c.tick(ins, "pe", 1, f"{tag}_pv{j}")

            def emit_recip(E, j):
                c.wait(E, "pe", f"{tag}_den{j}")
                E.tensor_scalar(S32[0:1, 0:512], den_slot, SINK, None, ALU.add)
                E.drain()
                E.reciprocal(S32[0:1, 512:1024], S32[0:1, 0:512])
                E.drain()
                E.tensor_copy(S_bf[0:1, 3072:3584], S32[0:1, 512:1024])
                E.drain()
                ins = E.tensor_sub(S_bf[0:1, 3584:4096], S32[0:1, 512:1024],
                                   S_bf[0:1, 3072:3584])
                c.tick(ins, "dv", 1, f"{tag}_recip{j}")

            def emit_norm(E, j):
                c.wait(E, "pe", f"{tag}_R{j}")
                E.tensor_copy(S32[:, 1024:1536], ps_b.ap()[:])
                E.drain()
                hh, qc = divmod(j, 4)
                dst = A_bf[:, hh * 2048 + qc * 512: hh * 2048 + (qc + 1) * 512]
                ins = E.scalar_tensor_tensor(dst, pv_bank(j), 1.0,
                                             S32[:, 1024:1536], ALU.mult, ALU.mult)
                c.tick(ins, "dv", 1, f"{tag}_norm{j}")

            if en == "pe":
                for j in range(NJ):
                    if j >= 2:
                        emit_R(E, j - 2)
                    if j >= 1:
                        emit_den(E, j - 1)
                    for kt in range(16):
                        if kt >= 4:
                            c.wait(E, "ac", f"{tag}_exp{j}_{(kt - 4) // 2}")
                        if j == 0 and kt == 0:
                            c.wait(E, "ac", "a4_qq3done")
                            c.wait(E, "ac", "a4_rms3_15")
                        ins = E.matmul(sT_bank(kt), kT_tile(j, kt), qT_rhs(j),
                                       start=True, stop=True)
                        c.tick(ins, "pe", 1, f"{tag}_sc{j}_{kt}")
                        if j >= 1:
                            emit_pv(E, j - 1, kt)
                emit_R(E, 62)
                emit_den(E, 63)
                for kt in range(16):
                    emit_pv(E, 63, kt)
                emit_R(E, 63)
                c.tick(None, "pe", 0, "a5_done")
            if en == "ac":
                for j in range(NJ):
                    for p in range(8):
                        if p == 0:
                            if j >= 2:
                                c.wait(E, "pe", f"{tag}_pv{j-2}")
                            if j == 0:
                                c.wait(E, "pe", "a4_mm2_15")
                            if j == 1:
                                c.wait(E, "pe", "a4_mm3_15")
                        c.wait(E, "pe", f"{tag}_sc{j}_{2 * p + 1}")
                        ins = E.activation(
                            C_bf[:, (j % 2) * 8192 + p * 1024:
                                 (j % 2) * 8192 + (p + 1) * 1024],
                            ps_big.ap()[:, (p % 2) * 1024:(p % 2) * 1024 + 1024],
                            AF.Exp)
                        c.tick(ins, "ac", 1, f"{tag}_exp{j}_{p}")
            if en == "dv":
                for j in range(NJ):
                    if j >= 2:
                        emit_norm(E, j - 2)
                    if j >= 1:
                        emit_recip(E, j - 1)
                emit_norm(E, 62)
                emit_recip(E, 63)
                emit_norm(E, 63)

        def ph_wo(en, E):
            tag = "a6"
            wosrc = dram["wo"].rearrange("(hc p) o -> p hc o", p=128)
            def _wo_load(E, j):
                if j < 2:
                    c.wait(E, "pe", "a5_sc63_15")
                    dst = B_bf[:, j * 8192:(j + 1) * 8192].rearrange(
                        "p (a b) -> p a b", b=512)
                else:
                    c.wait(E, "pe", f"{tag}_mm{j-2}_15")
                    c.wait(E, "pe", "a5_done")
                    dst = cslot_bf(j % 2).rearrange("p (a b) -> p a b", b=512)
                ins = E.dma_start(dst, wosrc[:, :, j * 512:(j + 1) * 512])
                c.tick(ins, f"w{j % 2}", 16, f"{tag}_w{j}")

            for j in range(4):
                slot = j % 2
                if en == "gp" and j == 0:
                    _wo_load(E, 0)
                    _wo_load(E, 1)
                for tt in range(NT):
                    k = j * NT + tt
                    r = k % 4
                    if en == "pe":
                        if tt == 0:
                            c.wait(E, f"w{j % 2}", f"{tag}_w{j}")
                            if j == 0:
                                c.wait(E, "dv", "a5_norm63")
                        if k >= 4:
                            c.wait(E, "dv", f"{tag}_ev{k-4}")
                        wsrc_bf = (B_bf[:, j * 8192:(j + 1) * 8192] if j < 2
                                   else cslot_bf(slot))
                        ins = None
                        for hc in range(ND):
                            ins = E.matmul(
                                ring_ap(r), AT_lhsT(hc, tt),
                                wsrc_bf[:, hc * 512:(hc + 1) * 512],
                                start=(hc == 0), stop=(hc == ND - 1))
                        c.tick(ins, "pe", 1, f"{tag}_mm{j}_{tt}")
                    if en == "dv":
                        c.wait(E, "pe", f"{tag}_mm{j}_{tt}")
                        if k >= 4:
                            pk = ((k - 4) // 2) * 2
                            c.wait(E, f"st{(pk // 2) % 2}", f"{tag}_stk{pk}")
                        dslot = S32[:, (k % 4) * 512:(k % 4) * 512 + 512]
                        ins = E.tensor_copy(dslot, ring_ap(r))
                        c.tick(ins, "dv", 1, f"{tag}_ev{k}")
                    if en == "gp":
                        if tt == 2 and j >= 1 and j < 3:
                            _wo_load(E, j + 1)
                        if tt % 2 == 1:
                            c.wait(E, "dv", f"{tag}_ev{k}")
                            base = ((k - 1) % 4) * 512
                            src = S32[:, base:base + 1024].rearrange(
                                "p (a b) -> p a b", b=512)
                            ins = E.dma_start(
                                y_view[:, tt - 1: tt + 1, j * 512:(j + 1) * 512],
                                src, accum_op=ALU.add)
                            c.tick(ins, f"st{((k - 1) // 2) % 2}", 16, f"{tag}_stk{k-1}")
            if en == "gp":
                c.tick(None, "st0", 0, "s1_acc_done")
                c.tick(None, "st1", 0, "s1_acc_done2")

        def ph_fin(en, E):
            if c.mode != "emit":
                return
            if en == "gp":
                for s in ("st0", "st1", "w0", "w1", "w2", "w3"):
                    E.wait_ge(c.handles[s], c.cnt[s])
            if en == "sp":
                for s in ("ini", "xe", "xo", "scs", "xp0", "xp1", "xp2", "xp3"):
                    E.wait_ge(c.handles[s], c.cnt[s])

        def run_all(en, E):
            c.reset()
            ph_init(en, E)
            ph_pre0(en, E)
            ph_hT(0, en, E)
            ph_copy(en, E)
            ph_up(0, en, E)
            ph_down(0, en, E)
            ph_hT(1, en, E)
            ph_kv(en, E)
            ph_q(en, E)
            ph_core(en, E)
            ph_wo(en, E)
            ph_hT(2, en, E)
            ph_up(2, en, E)
            ph_down(2, en, E)
            ph_fin(en, E)

        c.mode = "count"
        for en in ENGS:
            c.reset()
            run_all(en, _Null())
        c.check()

        c.mode = "emit"

        @block.sync
        def _(sync):
            run_all("sp", sync)

        @block.gpsimd
        def _(gpsimd):
            run_all("gp", gpsimd)

        @block.tensor
        def _(tensor):
            run_all("pe", tensor)

        @block.scalar
        def _(scalar):
            run_all("ac", scalar)

        @block.vector
        def _(vector):
            run_all("dv", vector)

    return nc


def kernel(**inputs):
    inp = {k: np.asarray(v) for k, v in inputs.items()}
    hs = np.ascontiguousarray(inp["L0_attn_hidden_states"], dtype=np.float32)
    common = {
        "cos": np.ascontiguousarray(inp["L0_attn_cos"][0, :2048, 0, :], dtype=np.float32),
        "sin": np.ascontiguousarray(inp["L0_attn_sin"][0, :2048, 0, :], dtype=np.float32),
        "wq": np.ascontiguousarray(inp["L0_attn_wq"], dtype=np.float32),
        "wk": np.ascontiguousarray(inp["L0_attn_wk"], dtype=np.float32),
        "wv": np.ascontiguousarray(inp["L0_attn_wv"], dtype=np.float32),
        "wo": np.ascontiguousarray(inp["L0_attn_wo"], dtype=np.float32),
        "wupT": np.ascontiguousarray(np.asarray(inp["L0_ffn_W_up"]).T, dtype=np.float32),
        "wdownT": np.ascontiguousarray(np.asarray(inp["L0_ffn_W_down"]).T, dtype=np.float32),
        "wexp": np.ascontiguousarray(inp["L0_ffn_W_expert"], dtype=np.float32),
        "qn": np.ascontiguousarray(inp["L0_attn_q_norm_w"].reshape(1, 128), dtype=np.float32),
        "kn": np.ascontiguousarray(inp["L0_attn_k_norm_w"].reshape(1, 128), dtype=np.float32),
        "identbf": np.eye(128, dtype=ml_dtypes.bfloat16),
        "ident32": np.eye(128, dtype=np.float32),
    }
    nc = build_nc()
    in_maps = [dict(common, x=np.ascontiguousarray(hs[b])) for b in range(8)]
    res = run_bass_kernel_spmd(nc, in_maps, core_ids=list(range(8)))
    out = np.stack([res.results[b]["y"] for b in range(8)], axis=0)
    return out.astype(np.float32)


if __name__ == "__main__":
    nc = build_nc()
    n = sum(1 for _ in nc.all_instructions())
    print("built ok, instructions:", n)
